# version 01
# baseline (speedup 1.0000x reference)
import sys
import numpy as np

sys.path.insert(0, "/opt/trn_rl_repo")

import concourse.bass as bass
import concourse.mybir as mybir
from concourse.bass_utils import run_bass_kernel_spmd

F32 = mybir.dt.float32
BF16 = mybir.dt.bfloat16
ALU = mybir.AluOpType
ACT = mybir.ActivationFunctionType
AXX = mybir.AxisListType.X

P = 128
D = 1024
G = 8
T = 2048
CH = 512
NCHUNK = T // CH
SEQ_PER_CORE = 2
N_CORES = 8
LN_EPS = 1e-5

PHASES = ("attn_a", "ffn1", "attnb", "ffn2")
DEBUG_TAG = None

ENGINES = ("tensor", "vector", "scalar", "gpsimd", "sync")


class Sem:
    def __init__(self, handle):
        self.h = handle
        self.n = 0


class Builder:

    def __init__(self, nc):
        self.nc = nc
        self.q = {e: [] for e in ENGINES}
        self.sems = {}

    def sem(self, name):
        s = Sem(self.nc.semaphore(name).__enter__())
        self.sems[name] = s
        return s

    def emit(self, eng, fn):
        self.q[eng].append(fn)

    def wait(self, eng, sem, val=None):
        v = sem.n if val is None else val
        if v <= 0:
            return
        self.emit(eng, lambda E, h=sem.h, v=v: E.wait_ge(h, v))

    def dwait(self, eng, sem, val=None, n=3):
        self.wait(eng, sem, val)
        sp = self.spacer_ap
        for _ in range(n):
            if eng == "scalar":
                self.emit(eng, lambda E, sp=sp: E.copy(sp[1], sp[0]))
            else:
                self.emit(eng, lambda E, sp=sp: E.tensor_copy(sp[1], sp[0]))

    def op(self, eng, fn, inc=None, n=1):
        if inc is None:
            self.emit(eng, fn)
            return None
        inc.n += n
        self.emit(eng, lambda E, fn=fn, h=inc.h, n=n: fn(E).then_inc(h, n))
        return inc.n

    def replay(self, block):
        for name in ENGINES:
            section = getattr(block, name)
            q = self.q[name]

            def runner(E, q=q):
                for fn in q:
                    fn(E)

            section(runner)


class Arena:

    def __init__(self, tensor, ncols):
        self.t = tensor
        self.ncols = ncols
        self.off = 0
        self.marks = []

    def take_f32(self, ncols):
        alloc = (ncols + 15) // 16 * 16
        assert self.off + alloc <= self.ncols, f"arena overflow: {self.off}+{alloc} > {self.ncols}"
        ap = self.t[:, self.off:self.off + ncols]
        self.off += alloc
        return ap

    def take_bf16(self, ncols):
        f32cols = (ncols + 1) // 2
        return self.take_f32(f32cols).bitcast(BF16)

    def push(self):
        self.marks.append(self.off)

    def pop(self):
        self.off = self.marks.pop()


class BankPool:

    def __init__(self, bld, banks):
        self.bld = bld
        self.banks = banks
        self.free_at = [None] * len(banks)
        self.rr = 0

    def acquire(self, eng="tensor"):
        for _ in range(len(self.banks)):
            i = self.rr % len(self.banks)
            self.rr += 1
            if not getattr(self, "in_use", None):
                self.in_use = [False] * len(self.banks)
            if self.in_use[i]:
                continue
            fa = self.free_at[i]
            if fa is not None:
                self.bld.wait(eng, fa[0], fa[1])
            self.free_at[i] = None
            self.in_use[i] = True
            return i, self.banks[i]
        raise AssertionError("all PSUM banks in use")

    def release(self, i, sem, val):
        self.free_at[i] = (sem, val)
        self.in_use[i] = False


def build_program(phases=None, debug_dump=None):
    phases = PHASES if phases is None else phases
    nc = bass.Bass("TRN2", target_bir_lowering=False, debug=False)

    x_in = nc.dram_tensor("x", [SEQ_PER_CORE, T, D], F32, kind="ExternalInput").ap()
    y_out = nc.dram_tensor("y", [SEQ_PER_CORE, T, D], F32, kind="ExternalOutput").ap()
    dbg_out = None
    if DEBUG_TAG is not None:
        dbg_out = nc.dram_tensor("dbg", [P, 8192], F32, kind="ExternalOutput").ap()

    def din(name, shape):
        return nc.dram_tensor(name, shape, F32, kind="ExternalInput").ap()

    wzh = din("wzh", [D, 2 * D])
    wg_a = din("wg_a", [D, D])
    vecs = din("vecs", [P, 8 * 16])
    idn = din("idn", [P, P])
    rw = din("rw", [D, 16])
    wgu_r = din("wgu_r", [16, D, 512])
    wd_r = din("wd_r", [16, 256, D])
    wgu_s = din("wgu_s", [2, D, 1024])
    wd_s = din("wd_s", [2, 512, D])
    onehot = din("onehot", [16, 2048])
    wkvr = din("wkvr", [D, 3 * D])
    wck = din("wck", [D, 4096])
    wcv = din("wcv", [4096, D])
    wcr = din("wcr", [D, D])

    bld = Builder(nc)
    WD = bld.sem("WD")
    XD = bld.sem("XD")
    OD = bld.sem("OD")
    PE = bld.sem("PE")
    V = bld.sem("V")
    S = bld.sem("S")
    GP = bld.sem("GP")

    x_fm = nc.alloc_sbuf_tensor("x_fm", [P, G * T], F32)
    w_ring = nc.alloc_sbuf_tensor("w_ring", [P, 24576], BF16)
    arena_t = nc.alloc_sbuf_tensor("arena", [P, 17408], F32)
    io_st = nc.alloc_sbuf_tensor("io_st", [P, 2 * D], F32)
    rows = nc.alloc_sbuf_tensor("rows", [P, 3072], F32)
    vec_sb = nc.alloc_sbuf_tensor("vec_sb", [P, 8 * 16 + 32], F32)
    idn_sb = nc.alloc_sbuf_tensor("idn_sb", [P, P], F32)
    ones_bf = nc.alloc_sbuf_tensor("ones_bf", [P, 2], BF16)
    ones_row = nc.alloc_sbuf_tensor("ones_row", [1, P], F32)
    spacer_sb = nc.alloc_sbuf_tensor("spacer_sb", [P, 64], F32)
    bld.spacer_ap = (spacer_sb[:, 0:16], spacer_sb[:, 16:32])
    rw_sb = nc.alloc_sbuf_tensor("rw_sb", [P, 128], BF16)
    onehot_sb = nc.alloc_sbuf_tensor("onehot_sb", [P, 2048], BF16)
    state_sb = nc.alloc_sbuf_tensor("state_sb", [P, 64], F32)

    psum = [nc.alloc_psum_tensor(f"ps{i}", [P, CH], F32) for i in range(8)]
    banks = BankPool(bld, psum)

    arena = Arena(arena_t, 17408)

    VEC_IDX = {n: i for i, n in enumerate(
        ["bz", "bh", "bg", "w", "u", "rk", "rv", "rr", "crk", "crr",
         "ln1g", "ln1b", "ln2g", "ln2b", "ew"])}

    def vcol(name, g):
        return vec_sb[:, VEC_IDX[name] * 8 + g: VEC_IDX[name] * 8 + g + 1]

    bld.op("sync", lambda E: E.dma_start(out=vec_sb[:, 0:128], in_=vecs), inc=XD, n=16)
    bld.op("sync", lambda E: E.dma_start(out=idn_sb[:], in_=idn), inc=XD, n=16)
    bld.op("gpsimd", lambda E: E.dma_start(out=onehot_sb[0:16, :], in_=onehot), inc=WD, n=16)
    eps_col = vec_sb[:, 128:129]
    bld.op("gpsimd", lambda E: E.memset(ones_bf[:], 1.0))
    bld.op("gpsimd", lambda E: E.memset(ones_row[:], 1.0))
    bld.op("gpsimd", lambda E: E.memset(eps_col, LN_EPS))
    bld.op("gpsimd", lambda E: E.memset(state_sb[:], 0.0), inc=GP)
    bld.wait("scalar", XD)
    bld.op("scalar", lambda E: E.activation(
        out=vec_sb[:, VEC_IDX["ew"] * 8:VEC_IDX["ew"] * 8 + 8],
        in_=vec_sb[:, VEC_IDX["w"] * 8:VEC_IDX["w"] * 8 + 8], func=ACT.Exp), inc=S)
    for _e in ENGINES:
        bld.wait(_e, XD)
        bld.wait(_e, GP)
        bld.wait(_e, S)
    for _ in range(4):
        bld.op("vector", lambda E: E.tensor_copy(spacer_sb[:, 16:32], spacer_sb[:, 0:16]))
        bld.op("scalar", lambda E: E.copy(spacer_sb[:, 32:48], spacer_sb[:, 0:16]))

    def load_seq(s):
        for ti in range(T // P):
            buf = io_st[:, (ti % 2) * D:(ti % 2) * D + D]
            bld.wait("sync", PE, PE.n)
            ld = bld.op("sync", lambda E, buf=buf, s=s, ti=ti: E.dma_start(
                out=buf, in_=x_in[s, ti * P:(ti + 1) * P, :]), inc=XD, n=16)
            bld.wait("tensor", XD, ld)
            pe_guard()
            for g in range(G):
                bi, bank = banks.acquire()
                pe_v = bld.op("tensor", lambda E, bank=bank, buf=buf, g=g: E.transpose(
                    bank[:, 0:P], buf[:, g * P:(g + 1) * P], idn_sb[:]), inc=PE)
                bld.wait("vector", PE, pe_v)
                v_v = bld.op("vector", lambda E, bank=bank, g=g, ti=ti: E.tensor_copy(
                    x_fm[:, g * T + ti * P: g * T + (ti + 1) * P], bank[:, 0:P]), inc=V)
                banks.release(bi, V, v_v)

    def store_seq(s):
        bld.wait("tensor", V, V.n)
        for ti in range(T // P):
            buf = io_st[:, (ti % 2) * D:(ti % 2) * D + D]
            if ti >= 2:
                bld.wait("vector", OD, (store_seq.count - 1) * 16)
            for g in range(G):
                bi, bank = banks.acquire()
                pe_v = bld.op("tensor", lambda E, bank=bank, g=g, ti=ti: E.transpose(
                    bank[:, 0:P], x_fm[:, g * T + ti * P: g * T + (ti + 1) * P], idn_sb[:]),
                    inc=PE)
                bld.wait("vector", PE, pe_v)
                v_v = bld.op("vector", lambda E, bank=bank, buf=buf, g=g: E.tensor_copy(
                    buf[:, g * P:(g + 1) * P], bank[:, 0:P]), inc=V)
                banks.release(bi, V, v_v)
            bld.wait("sync", V, V.n)
            bld.op("sync", lambda E, buf=buf, s=s, ti=ti: E.dma_start(
                out=y_out[s, ti * P:(ti + 1) * P, :], in_=buf), inc=OD, n=16)
            store_seq.count += 1
    store_seq.count = 0

    def pe_guard(n=8):
        for _ in range(n):
            bld.op("tensor", lambda E: E.ldweights(ones_bf[:, 0:1]))

    dbg_state = {"done": False}

    def dbg_dump(tag, ap_f32, cols):
        if DEBUG_TAG != tag or dbg_state["done"]:
            return
        dbg_state["done"] = True
        bld.wait("sync", V, V.n)
        bld.wait("sync", S, S.n)
        bld.op("sync", lambda E: E.dma_start(out=dbg_out[:, 0:cols], in_=ap_f32), inc=OD, n=16)
        for _e in ("vector", "scalar", "tensor", "gpsimd"):
            bld.wait(_e, OD, OD.n)

    wring_half = [None, None]

    def wait_halves(eng, halves):
        for h in halves:
            if wring_half[h] is not None:
                bld.wait(eng, wring_half[h][0], wring_half[h][1])
                wring_half[h] = None

    def wload(dst_cols, src_ap):
        wait_halves("gpsimd", (0, 1))
        return bld.op("gpsimd", lambda E, d=dst_cols, s=src_ap: E.dma_start(
            out=w_ring[:, d[0]:d[1]], in_=s), inc=WD, n=16)

    def mark_wring_free():
        wring_half[0] = (PE, PE.n)
        wring_half[1] = (PE, PE.n)

    def ln_cast(s, c, xbf_b, tmp_f, dest_fn):
        xc = lambda g: x_fm[:, g * T + c * CH: g * T + (c + 1) * CH]
        for g in range(G):
            bld.op("vector", lambda E, g=g: E.tensor_copy(
                dest_fn(g), xc(g)), inc=(V if g == G - 1 else None))
        bld.dwait("scalar", V, V.n)
        for g in range(G):
            bld.op("scalar", lambda E, g=g: E.activation(
                out=xbf_b[:, g * CH:(g + 1) * CH], in_=xc(g), func=ACT.Square),
                inc=(S if g == G - 1 else None))
        b1i, b1 = banks.acquire()
        bld.wait("tensor", V, V.n)
        for g in range(G):
            bld.op("tensor", lambda E, g=g, b1=b1: E.matmul(
                b1[0:1, 0:CH], ones_bf[:, 0:1], dest_fn(g),
                start=(g == 0), stop=(g == G - 1)), inc=(PE if g == G - 1 else None))
        b2i, b2 = banks.acquire()
        bld.wait("tensor", S, S.n)
        for g in range(G):
            bld.op("tensor", lambda E, g=g, b2=b2: E.matmul(
                b2[0:1, 0:CH], ones_bf[:, 0:1], xbf_b[:, g * CH:(g + 1) * CH],
                start=(g == 0), stop=(g == G - 1)), inc=(PE if g == G - 1 else None))
        mu_row = rows[0:1, 0:CH]
        r_row = rows[0:1, CH:2 * CH]
        t_row = rows[0:1, 2 * CH:3 * CH]
        nm_row = rows[0:1, 3 * CH:4 * CH]
        bld.wait("vector", PE, PE.n)
        bld.op("vector", lambda E: E.tensor_scalar_mul(mu_row, b1[0:1, 0:CH], 1.0 / D))
        v1 = bld.op("vector", lambda E: E.tensor_scalar_mul(t_row, b2[0:1, 0:CH], 1.0 / D), inc=V)
        banks.release(b1i, V, v1)
        banks.release(b2i, V, v1)
        bld.op("vector", lambda E: E.tensor_mul(nm_row, mu_row, mu_row))
        bld.op("vector", lambda E: E.tensor_sub(t_row, t_row, nm_row), inc=V)
        bld.wait("scalar", V, V.n)
        bld.op("scalar", lambda E: E.activation(out=t_row, in_=t_row, func=ACT.Ln, bias=eps_col[0:1, :]))
        bld.op("scalar", lambda E: E.activation(out=r_row, in_=t_row, func=ACT.Exp, scale=-0.5), inc=S)
        bld.dwait("vector", S, S.n)
        bld.op("vector", lambda E: E.scalar_tensor_tensor(
            out=nm_row, in0=mu_row, scalar=-1.0, in1=r_row, op0=ALU.mult, op1=ALU.mult), inc=V)
        bri, bank_r = banks.acquire()
        bld.wait("tensor", V, V.n)
        bld.op("tensor", lambda E, bank_r=bank_r: E.matmul(
            bank_r[:, 0:CH], ones_row[0:1, :], r_row, start=True, stop=True))
        bmi, bank_m = banks.acquire()
        bld.op("tensor", lambda E, bank_m=bank_m: E.matmul(
            bank_m[:, 0:CH], ones_row[0:1, :], nm_row, start=True, stop=True), inc=PE)
        bld.wait("vector", PE, PE.n)
        for g in range(G):
            bld.op("vector", lambda E, g=g, bank_r=bank_r: E.tensor_mul(
                tmp_f, xc(g), bank_r[:, 0:CH]))
            bld.op("vector", lambda E, g=g, bank_m=bank_m: E.tensor_add(
                dest_fn(g), tmp_f, bank_m[:, 0:CH]),
                inc=(V if g == G - 1 else None))
        banks.release(bri, V, V.n)
        banks.release(bmi, V, V.n)
        return r_row, mu_row, V.n

    def phase_attn_a(s):
        arena.push()
        CH1 = CH + 1
        xbf_a = arena.take_bf16(G * CH)
        xbf_b = arena.take_bf16(G * CH)
        tmp_f = arena.take_f32(CH)
        h_full = arena.take_f32(G * CH1)
        zslot = [arena.take_f32(CH) for _ in range(2)]
        aslot = [arena.take_f32(CH1) for _ in range(2)]
        bslot = [arena.take_f32(CH1) for _ in range(2)]
        gslot = [arena.take_f32(CH) for _ in range(2)]
        hbf = arena.take_bf16(G * CH)

        for k in range(G):
            wload((k * 2048, (k + 1) * 2048), wzh[k * P:(k + 1) * P, :])
        for k in range(G):
            wload((16384 + k * 1024, 16384 + (k + 1) * 1024), wg_a[k * P:(k + 1) * P, :])
        wd_done = WD.n

        bld.op("vector", lambda E: E.memset(aslot[0][:, 0:1], 0.0))
        bld.op("vector", lambda E: E.memset(aslot[1][:, 0:1], 0.0), inc=V)

        dbgrow = arena.take_f32(16)
        zuse = [0, 0]
        guse = [0, 0]
        x_bf = xbf_a
        for c in range(NCHUNK):
            _, _, v_xbf = ln_cast(s, c, xbf_b, tmp_f,
                                  dest_fn=lambda g: xbf_a[:, g * CH:(g + 1) * CH])
            bld.wait("tensor", WD, wd_done)
            bld.wait("tensor", V, v_xbf)
            pe_guard()
            for m in range(G):
                zs, as_, bs = zslot[m % 2], aslot[m % 2], bslot[m % 2]
                hsl = h_full[:, m * CH1:(m + 1) * CH1]
                if c == 0:
                    bld.op("vector", lambda E, bs=bs: E.memset(bs[:, 0:1], 0.0))
                else:
                    bld.op("vector", lambda E, bs=bs, hsl=hsl: E.tensor_copy(
                        bs[:, 0:1], hsl[:, CH:CH + 1]))
                bzi, bank_z = banks.acquire()
                for k in range(G):
                    bld.op("tensor", lambda E, k=k, m=m, bank_z=bank_z: E.matmul(
                        bank_z[:, 0:CH], w_ring[:, k * 2048 + m * P: k * 2048 + (m + 1) * P],
                        x_bf[:, k * CH:(k + 1) * CH], start=(k == 0), stop=(k == G - 1)),
                        inc=(PE if k == G - 1 else None))
                pe_z = PE.n
                bhi, bank_h = banks.acquire()
                for k in range(G):
                    bld.op("tensor", lambda E, k=k, m=m, bank_h=bank_h: E.matmul(
                        bank_h[:, 0:CH], w_ring[:, k * 2048 + 1024 + m * P: k * 2048 + 1024 + (m + 1) * P],
                        x_bf[:, k * CH:(k + 1) * CH], start=(k == 0), stop=(k == G - 1)),
                        inc=(PE if k == G - 1 else None))
                pe_h = PE.n
                bld.wait("scalar", PE, pe_z)
                bld.wait("scalar", V, zuse[m % 2])
                s_z = bld.op("scalar", lambda E, zs=zs, m=m, bank_z=bank_z: E.activation(
                    out=zs, in_=bank_z[:, 0:CH], func=ACT.Sigmoid, bias=vcol("bz", m)), inc=S)
                banks.release(bzi, S, s_z)
                bld.dwait("vector", S, s_z)
                bld.op("vector", lambda E, as_=as_, zs=zs: E.tensor_scalar(
                    out=as_[:, 1:], in0=zs, scalar1=-1.0, scalar2=1.0, op0=ALU.mult, op1=ALU.add))
                bld.wait("vector", PE, pe_h)
                v_b = bld.op("vector", lambda E, bs=bs, zs=zs, m=m, bank_h=bank_h: E.scalar_tensor_tensor(
                    out=bs[:, 1:], in0=bank_h[:, 0:CH], scalar=vcol("bh", m), in1=zs,
                    op0=ALU.add, op1=ALU.mult), inc=V)
                banks.release(bhi, V, v_b)
                zuse[m % 2] = v_b
                bld.op("vector", lambda E, as_=as_, bs=bs, hsl=hsl: E.tensor_tensor_scan(
                    out=hsl, data0=as_, data1=bs,
                    initial=0.0, op0=ALU.mult, op1=ALU.add))
                bld.op("vector", lambda E, m=m, hsl=hsl: E.tensor_copy(
                    hbf[:, m * CH:(m + 1) * CH], hsl[:, 1:]),
                    inc=V)
            v_hready = V.n
            bld.wait("tensor", V, v_hready)
            for m in range(G):
                gsl = gslot[m % 2]
                bgi, bank_g = banks.acquire()
                for k in range(G):
                    bld.op("tensor", lambda E, k=k, m=m, bank_g=bank_g: E.matmul(
                        bank_g[:, 0:CH], w_ring[:, 16384 + k * 1024 + m * P: 16384 + k * 1024 + (m + 1) * P],
                        hbf[:, k * CH:(k + 1) * CH], start=(k == 0), stop=(k == G - 1)),
                        inc=(PE if k == G - 1 else None))
                pe_g = PE.n
                bld.wait("scalar", PE, pe_g)
                bld.wait("scalar", V, guse[m % 2])
                s_g = bld.op("scalar", lambda E, gsl=gsl, m=m, bank_g=bank_g: E.activation(
                    out=gsl, in_=bank_g[:, 0:CH], func=ACT.Sigmoid, bias=vcol("bg", m)), inc=S)
                banks.release(bgi, S, s_g)
                bld.dwait("vector", S, s_g)
                bld.op("vector", lambda E, gsl=gsl, m=m: E.tensor_mul(
                    gsl, gsl, h_full[:, m * CH1 + 1:(m + 1) * CH1]))
                guse[m % 2] = bld.op("vector", lambda E, gsl=gsl, m=m, c=c: E.tensor_add(
                    x_fm[:, m * T + c * CH: m * T + (c + 1) * CH],
                    x_fm[:, m * T + c * CH: m * T + (c + 1) * CH], gsl), inc=V)
        mark_wring_free()
        arena.pop()

    RING_H = 12288

    def wload_half(h, loads):
        wait_halves("gpsimd", (h,))
        base = h * RING_H
        for (dst, ncols, src) in loads:
            k = src.shape[1] if src.ndim == 3 else None
            bld.op("gpsimd", lambda E, dst=dst, ncols=ncols, src=src, k=k: E.dma_start(
                out=(w_ring[:, base + dst:base + dst + ncols].rearrange(
                    "p (k m) -> p k m", k=k) if k else w_ring[:, base + dst:base + dst + ncols]),
                in_=src), inc=WD, n=16)
        return WD.n

    def spacer(eng, n=2):
        for _ in range(n):
            if eng == "scalar":
                bld.op(eng, lambda E: E.copy(spacer_sb[:, 32:64], spacer_sb[:, 0:32]))
            else:
                bld.op(eng, lambda E: E.tensor_copy(spacer_sb[:, 32:64], spacer_sb[:, 0:32]))

    def phase_ffn(s):
        arena.push()
        xn_bf = arena.take_bf16(G * T)
        comb_fm = arena.take_bf16(T)
        arena.push()
        xbf_b = arena.take_bf16(G * CH)
        tmp_f = arena.take_f32(CH)
        lt_fm = arena.take_f32(CH)
        lt_tm = arena.take_f32(16 * 16)
        el_all = arena.take_f32(16 * 16)
        el2 = arena.take_f32(16 * 16)
        mx8_all = arena.take_f32(16 * 8)
        r16a = arena.take_f32(16)
        r16b = arena.take_f32(16)
        r16c = arena.take_f32(16)
        r16d = arena.take_f32(16)
        r16e = arena.take_f32(16)
        r16f = arena.take_f32(16)
        r16g = arena.take_f32(16)

        bld.op("gpsimd", lambda E: E.dma_start(
            out=rw_sb[:, 0:128].rearrange("p (k m) -> p k m", k=8),
            in_=rw.rearrange("(k p) m -> p k m", p=128)), inc=WD, n=16)
        wd_rw = WD.n

        for c in range(NCHUNK):
            ln_cast(s, c, xbf_b, tmp_f,
                    dest_fn=lambda g, c=c: xn_bf[:, g * T + c * CH: g * T + (c + 1) * CH])
        v_xn = V.n

        def xn_mv(k, n):
            return xn_bf[:, k * T + n * CH:(k) * T + (n + 1) * CH]

        bld.wait("tensor", WD, wd_rw)
        bld.wait("tensor", V, v_xn)
        pe_guard()
        for n in range(NCHUNK):
            bli, bank_l = banks.acquire()
            for k in range(G):
                bld.op("tensor", lambda E, k=k, n=n, bank_l=bank_l: E.matmul(
                    bank_l[0:16, 0:CH], rw_sb[:, k * 16:(k + 1) * 16], xn_mv(k, n),
                    start=(k == 0), stop=(k == G - 1)), inc=(PE if k == G - 1 else None))
            bld.wait("vector", PE, PE.n)
            v_l = bld.op("vector", lambda E, bank_l=bank_l: E.tensor_copy(
                lt_fm[0:16, 0:CH], bank_l[0:16, 0:CH]), inc=V)
            banks.release(bli, V, v_l)
            bld.wait("tensor", V, v_l)
            pe_guard()
            for tt in range(4):
                bti, bank_t = banks.acquire()
                pe_t = bld.op("tensor", lambda E, tt=tt, bank_t=bank_t: E.transpose(
                    bank_t[0:P, 0:16], lt_fm[0:16, tt * P:(tt + 1) * P],
                    idn_sb[0:16, 0:16]), inc=PE)
                bld.wait("vector", PE, pe_t)
                i = n * 4 + tt
                v_t = bld.op("vector", lambda E, i=i, bank_t=bank_t: E.tensor_copy(
                    lt_tm[:, i * 16:(i + 1) * 16], bank_t[:, 0:16]), inc=V)
                banks.release(bti, V, v_t)
        if s == 0:
            dbg_dump("ffn_lt", lt_tm, 256)
        bld.op("vector", lambda E: E.memset(r16g, 0.0))
        for i in range(16):
            bld.op("vector", lambda E, i=i: E.max(
                out=mx8_all[:, i * 8:(i + 1) * 8], in_=lt_tm[:, i * 16:(i + 1) * 16]))
        spacer("vector", 3)
        bld.op("vector", lambda E: E.tensor_sub(
            r16a, r16g, mx8_all[:, 0:128:8]))
        bld.op("vector", lambda E: E.tensor_sub(
            r16b, mx8_all[:, 1:128:8], mx8_all[:, 0:128:8]), inc=V)
        spacer("vector", 2)
        for i in range(16):
            bld.op("vector", lambda E, i=i: E.tensor_single_scalar(
                out=el_all[:, i * 16:(i + 1) * 16], in_=lt_tm[:, i * 16:(i + 1) * 16],
                scalar=r16a[:, i:i + 1], op=ALU.add), inc=(V if i == 15 else None))
        bld.wait("scalar", V, V.n)
        bld.op("scalar", lambda E: E.activation(out=r16g, in_=r16b, func=ACT.Exp))
        s_el = bld.op("scalar", lambda E: E.activation(out=el2, in_=el_all, func=ACT.Exp), inc=S)
        el_all = el2
        bld.dwait("vector", S, s_el)
        for i in range(16):
            bld.op("vector", lambda E, i=i: E.reduce_sum(
                out=r16c[:, i:i + 1], in_=el_all[:, i * 16:(i + 1) * 16], axis=AXX))
        spacer("vector", 3)
        bld.op("vector", lambda E: E.tensor_scalar(
            out=r16e, in0=r16c, scalar1=1e-9, scalar2=1.0, op0=ALU.mult, op1=ALU.add))
        spacer("vector", 3)
        bld.op("vector", lambda E: E.tensor_add(r16f, r16e, r16g))
        spacer("vector", 3)
        bld.op("vector", lambda E: E.reciprocal(out=r16d, in_=r16f))
        spacer("vector", 3)
        if s == 0 and DEBUG_TAG in ("ffn_rows", "ffn_rows2"):
            bld.op("vector", lambda E: E.tensor_copy(el_all[:, 0:16], r16a))
            bld.op("vector", lambda E: E.tensor_copy(el_all[:, 16:32], r16b))
            bld.op("vector", lambda E: E.tensor_copy(el_all[:, 32:48], r16c))
            bld.op("vector", lambda E: E.tensor_copy(el_all[:, 48:64], r16g))
            bld.op("vector", lambda E: E.tensor_copy(el_all[:, 64:80], r16e))
            bld.op("vector", lambda E: E.tensor_copy(el_all[:, 80:96], r16f))
            bld.op("vector", lambda E: E.tensor_copy(el_all[:, 96:112], r16d), inc=V)
            dbg_dump(DEBUG_TAG if DEBUG_TAG in ("ffn_rows","ffn_rows2") else "ffn_rows", el_all[:, 0:112], 112)
        comb_all = arena.take_f32(16 * 16)
        for i in range(16):
            bld.op("vector", lambda E, i=i: E.tensor_single_scalar(
                out=comb_all[:, i * 16:(i + 1) * 16], in_=lt_tm[:, i * 16:(i + 1) * 16],
                scalar=mx8_all[:, i * 8 + 1:i * 8 + 2], op=ALU.is_ge))
        for i in range(16):
            bld.op("vector", lambda E, i=i: E.tensor_mul(
                comb_all[:, i * 16:(i + 1) * 16], el_all[:, i * 16:(i + 1) * 16],
                comb_all[:, i * 16:(i + 1) * 16]))
        for i in range(16):
            bld.op("vector", lambda E, i=i: E.tensor_single_scalar(
                out=comb_all[:, i * 16:(i + 1) * 16], in_=comb_all[:, i * 16:(i + 1) * 16],
                scalar=r16d[:, i:i + 1], op=ALU.mult), inc=(V if i == 15 else None))
        bld.wait("tensor", V, V.n)
        pe_guard()
        for i in range(16):
            bci, bank_c = banks.acquire()
            pe_c = bld.op("tensor", lambda E, i=i, bank_c=bank_c: E.transpose(
                bank_c[0:16, 0:P], comb_all[:, i * 16:(i + 1) * 16], idn_sb[:]), inc=PE)
            bld.wait("vector", PE, pe_c)
            v_cf = bld.op("vector", lambda E, i=i, bank_c=bank_c: E.tensor_copy(
                comb_fm[0:16, i * P:(i + 1) * P], bank_c[0:16, 0:P]), inc=V)
            banks.release(bci, V, v_cf)
        v_comb = V.n
        if s == 0:
            dbg_dump("ffn_el", el_all, 256)
            dbg_dump("ffn_r16f", r16f, 16)
            dbg_dump("ffn_e21", r16g, 16)
            dbg_dump("ffn_rden", r16d, 16)
        if s == 0 and DEBUG_TAG == "ffn_comb":
            bld.op("vector", lambda E: E.tensor_copy(el_all, comb_fm[0:16, 0:256].bitcast(F32).bitcast(BF16)) if False else E.tensor_copy(lt_tm[0:16, 0:256], comb_fm[0:16, 0:256]), inc=V)
            dbg_dump("ffn_comb", lt_tm, 256)
        arena.pop()
        arena.push()
        h_unit = arena.take_bf16(4 * T)
        t1 = [arena.take_f32(CH) for _ in range(2)]
        t1use = [0, 0]

        units = [("s", [0]), ("s", [1])] + [("r", [2 * p, 2 * p + 1]) for p in range(8)]

        def unit_loads(kind, experts):
            loads = []
            if kind == "r":
                for j, e in enumerate(experts):
                    loads.append((j * 4096, 4096, wgu_r[e].rearrange("(k p) m -> p k m", p=128)))
                for j, e in enumerate(experts):
                    loads.append((8192 + j * 2048, 2048, wd_r[e].rearrange("(k p) m -> p k m", p=128)))
            else:
                e = experts[0]
                loads.append((0, 8192, wgu_s[e].rearrange("(k p) m -> p k m", p=128)))
                loads.append((8192, 4096, wd_s[e].rearrange("(k p) m -> p k m", p=128)))
            return loads

        wd_units = []
        for u in range(2):
            wd_units.append(wload_half(u % 2, unit_loads(*units[u])))

        for u, (kind, experts) in enumerate(units):
            half = u % 2
            base = half * RING_H
            bld.wait("tensor", WD, wd_units[u])
            pe_guard()
            if DEBUG_TAG == "ffn_ring" and s == 0 and u == 0:
                bld.wait("sync", WD, wd_units[u])
                dbg_dump("ffn_ring", w_ring[:, 0:16384].bitcast(F32), 8192)
            if u == 0:
                bld.wait("tensor", V, v_comb)
            nmid = 2 if kind == "r" else 4
            for j, e in enumerate(experts):
                gu_base = base + j * 4096 if kind == "r" else base
                wd_base = base + 8192 + j * 2048 if kind == "r" else base + 8192
                for n in range(NCHUNK):
                    bank_cb = None
                    if kind == "r":
                        bcbi, bank_cb = banks.acquire()
                        bld.op("tensor", lambda E, e=e, n=n, bank_cb=bank_cb: E.matmul(
                            bank_cb[:, 0:CH], onehot_sb[0:16, e * P:(e + 1) * P],
                            comb_fm[0:16, n * CH:(n + 1) * CH], start=True, stop=True),
                            inc=PE)
                    for mh in range(nmid):
                        bgi, bank_g = banks.acquire()
                        for k in range(G):
                            bld.op("tensor", lambda E, k=k, mh=mh, n=n, gu_base=gu_base, nmid=nmid, bank_g=bank_g: E.matmul(
                                bank_g[:, 0:CH],
                                w_ring[:, gu_base + k * (nmid * 256) + mh * P: gu_base + k * (nmid * 256) + (mh + 1) * P],
                                xn_mv(k, n), start=(k == 0), stop=(k == G - 1)),
                                inc=(PE if k == G - 1 else None))
                        pe_g = PE.n
                        bui, bank_u = banks.acquire()
                        for k in range(G):
                            bld.op("tensor", lambda E, k=k, mh=mh, n=n, gu_base=gu_base, nmid=nmid, bank_u=bank_u: E.matmul(
                                bank_u[:, 0:CH],
                                w_ring[:, gu_base + k * (nmid * 256) + (nmid + mh) * P: gu_base + k * (nmid * 256) + (nmid + mh + 1) * P],
                                xn_mv(k, n), start=(k == 0), stop=(k == G - 1)),
                                inc=(PE if k == G - 1 else None))
                        pe_u = PE.n
                        slot = (n * nmid + mh) % 2
                        ts = t1[slot]
                        bld.wait("scalar", PE, pe_g)
                        bld.wait("scalar", V, t1use[slot])
                        s_t1 = bld.op("scalar", lambda E, ts=ts, bank_g=bank_g: E.activation(
                            out=ts, in_=bank_g[:, 0:CH], func=ACT.Silu), inc=S)
                        banks.release(bgi, S, s_t1)
                        if DEBUG_TAG == "ffn_t1" and s == 0 and u == 0 and n == 0 and mh == 0:
                            dbg_dump("ffn_t1", ts, CH)
                        if DEBUG_TAG in ("ffn_gps","ffn_gps2") and s == 0 and u == 0 and n == 0 and mh == 0:
                            gf32 = arena.take_f32(CH)
                            bld.wait("vector", PE, pe_g)
                            bld.op("vector", lambda E, gf32=gf32, bank_g=bank_g: E.tensor_copy(gf32, bank_g[:, 0:CH]), inc=V)
                            dbg_dump(DEBUG_TAG, gf32, CH)
                        bld.dwait("vector", S, s_t1)
                        if kind == "r":
                            bld.op("vector", lambda E, ts=ts, bank_cb=bank_cb: E.tensor_mul(
                                ts, ts, bank_cb[:, 0:CH]))
                        bld.wait("vector", PE, pe_u)
                        gm = j * 2 + mh if kind == "r" else mh
                        v_h = bld.op("vector", lambda E, ts=ts, gm=gm, n=n, bank_u=bank_u: E.tensor_mul(
                            h_unit[:, gm * T + n * CH: gm * T + (n + 1) * CH],
                            ts, bank_u[:, 0:CH]), inc=V)
                        banks.release(bui, V, v_h)
                        t1use[slot] = v_h
                    if bank_cb is not None:
                        banks.release(bcbi, V, V.n)
            v_hdone = V.n
            if s == 0 and (DEBUG_TAG or "").startswith("ffn_h") and u == int(DEBUG_TAG[5:6]):
                hf32 = arena.take_f32(2048)
                bld.op("vector", lambda E, hf32=hf32: E.tensor_copy(hf32, h_unit[:, 0:T]), inc=V)
                dbg_dump(DEBUG_TAG, hf32, 2048)
            bld.wait("tensor", V, v_hdone)
            for o in range(G):
                for n in range(NCHUNK):
                    bdi, bank_d = banks.acquire()
                    for k in range(4):
                        bld.op("tensor", lambda E, k=k, o=o, n=n, base=base, bank_d=bank_d: E.matmul(
                            bank_d[:, 0:CH],
                            w_ring[:, base + 8192 + k * 1024 + o * P: base + 8192 + k * 1024 + (o + 1) * P],
                            h_unit[:, k * T + n * CH: k * T + (n + 1) * CH],
                            start=(k == 0), stop=(k == 3)),
                            inc=(PE if k == 3 else None))
                    pe_d = PE.n
                    bld.wait("vector", PE, pe_d)
                    v_d = bld.op("vector", lambda E, o=o, n=n, bank_d=bank_d: E.tensor_add(
                        x_fm[:, o * T + n * CH: o * T + (n + 1) * CH],
                        x_fm[:, o * T + n * CH: o * T + (n + 1) * CH],
                        bank_d[:, 0:CH]), inc=V)
                    banks.release(bdi, V, v_d)
            wring_half[half] = (PE, PE.n)
            if u + 2 < len(units):
                wd_units.append(wload_half((u + 2) % 2, unit_loads(*units[u + 2])))
        arena.pop()
        arena.pop()

    def attb_ln_rows(c0, CHB, xpb, xsb, tmp_f, double_ln, src_fn=None, want_r1=False):
        if src_fn is None:
            xc = lambda g: x_fm[:, g * T + c0: g * T + c0 + CHB]
            for g in range(G):
                bld.op("vector", lambda E, g=g: E.tensor_copy(
                    xpb[:, g * CHB:(g + 1) * CHB], xc(g)), inc=(V if g == G - 1 else None))
            stats_src = lambda g: xpb[:, g * CHB:(g + 1) * CHB]
            sq_src = xc
        else:
            stats_src = src_fn
            sq_src = src_fn
        bld.dwait("scalar", V, V.n)
        for g in range(G):
            bld.op("scalar", lambda E, g=g: E.activation(
                out=xsb[:, g * CHB:(g + 1) * CHB], in_=sq_src(g), func=ACT.Square),
                inc=(S if g == G - 1 else None))
        b1i, b1 = banks.acquire()
        bld.wait("tensor", V, V.n)
        for g in range(G):
            bld.op("tensor", lambda E, g=g, b1=b1: E.matmul(
                b1[0:1, 0:CHB], ones_bf[:, 0:1], stats_src(g),
                start=(g == 0), stop=(g == G - 1)), inc=(PE if g == G - 1 else None))
        b2i, b2 = banks.acquire()
        bld.wait("tensor", S, S.n)
        for g in range(G):
            bld.op("tensor", lambda E, g=g, b2=b2: E.matmul(
                b2[0:1, 0:CHB], ones_bf[:, 0:1], xsb[:, g * CHB:(g + 1) * CHB],
                start=(g == 0), stop=(g == G - 1)), inc=(PE if g == G - 1 else None))
        mu = rows[0:1, 0:CHB]
        v1 = rows[0:1, CHB:2 * CHB]
        t1 = rows[0:1, 2 * CHB:3 * CHB]
        s_row = rows[0:1, 3 * CHB:4 * CHB]
        ms_row = rows[0:1, 4 * CHB:5 * CHB]
        t2 = rows[0:1, 5 * CHB:6 * CHB]
        bld.wait("vector", PE, PE.n)
        bld.op("vector", lambda E: E.tensor_scalar_mul(mu, b1[0:1, 0:CHB], 1.0 / D))
        vv = bld.op("vector", lambda E: E.tensor_scalar_mul(v1, b2[0:1, 0:CHB], 1.0 / D), inc=V)
        banks.release(b1i, V, vv)
        banks.release(b2i, V, vv)
        bld.op("vector", lambda E: E.tensor_mul(t1, mu, mu))
        bld.op("vector", lambda E: E.tensor_sub(v1, v1, t1), inc=V)
        bld.dwait("scalar", V, V.n)
        bld.op("scalar", lambda E: E.activation(out=t1, in_=v1, func=ACT.Ln, bias=eps_col[0:1, :]))
        out = {}
        if double_ln:
            r1_row = rows[0:1, 6 * CHB:7 * CHB]
            mr1_row = rows[0:1, 7 * CHB:8 * CHB]
            bld.op("scalar", lambda E, r1_row=r1_row: E.activation(
                out=r1_row, in_=t1, func=ACT.Exp, scale=-0.5))
            s1v = bld.op("scalar", lambda E: E.activation(out=t2, in_=t1, func=ACT.Exp, scale=-1.0), inc=S)
            bld.dwait("vector", S, s1v)
            vv2 = bld.op("vector", lambda E: E.tensor_mul(t2, v1, t2), inc=V)
            bld.dwait("scalar", V, vv2)
            bld.op("scalar", lambda E: E.activation(out=t2, in_=t2, func=ACT.Ln, bias=eps_col[0:1, :]))
            s2v = bld.op("scalar", lambda E: E.activation(out=t2, in_=t2, func=ACT.Exp, scale=-0.5), inc=S)
            bld.dwait("vector", S, s2v)
            bld.op("vector", lambda E, r1_row=r1_row: E.tensor_mul(s_row, r1_row, t2))
            if want_r1:
                bld.op("vector", lambda E, r1_row=r1_row, mr1_row=mr1_row: E.scalar_tensor_tensor(
                    out=mr1_row, in0=mu, scalar=-1.0, in1=r1_row, op0=ALU.mult, op1=ALU.mult))
        else:
            s2v = bld.op("scalar", lambda E: E.activation(out=s_row, in_=t1, func=ACT.Exp, scale=-0.5), inc=S)
            bld.dwait("vector", S, s2v)
        bld.op("vector", lambda E: E.scalar_tensor_tensor(
            out=ms_row, in0=mu, scalar=-1.0, in1=s_row, op0=ALU.mult, op1=ALU.mult), inc=V)
        bld.wait("tensor", V, V.n)
        for name, row in ([("s", s_row), ("m", ms_row)] +
                          ([("r1", r1_row), ("mr1", mr1_row)] if (double_ln and want_r1) else [])):
            bi, bank = banks.acquire()
            bld.op("tensor", lambda E, bank=bank, row=row, CHB=CHB: E.matmul(
                bank[:, 0:CHB], ones_row[0:1, :], row, start=True, stop=True),
                inc=(PE if name == ("mr1" if (double_ln and want_r1) else "m") else None))
            out[name] = (bi, bank)
        return out

    def phase_attnb(s):
        CHB = 256
        L1 = CHB + 1
        HT = 512
        arena.push()
        x1_bf = arena.take_bf16(G * HT)

        pp_st = state_sb[:, 0:8]
        aa_st = state_sb[:, 8:16]
        kmaxc = state_sb[:, 16:24]
        mcol = state_sb[:, 24:32]
        negm = state_sb[:, 32:40]
        argc = state_sb[:, 40:48]
        rpc = state_sb[:, 48:56]

        bld.wait("gpsimd", V, V.n)
        bld.op("gpsimd", lambda E: E.memset(pp_st, -1e38))
        bld.op("gpsimd", lambda E: E.memset(aa_st, 0.0), inc=GP)
        bld.dwait("vector", GP, GP.n)

        for c5 in range(T // HT):
            arena.push()
            xa_pad = arena.take_f32(G * L1)
            xpb = arena.take_bf16(G * CHB)
            xsb = arena.take_bf16(G * CHB)
            tmp_f = arena.take_f32(CHB)
            dbuf = tmp_f
            xkb = arena.take_bf16(G * CHB)
            xvb = arena.take_bf16(G * CHB)
            xrb = arena.take_bf16(G * CHB)
            kbuf = arena.take_f32(G * CHB)
            vbuf = arena.take_bf16(G * CHB)
            sgbuf = arena.take_bf16(G * CHB)
            ewt = arena.take_f32(L1)
            kmb = arena.take_f32(CHB)
            d1b = arena.take_f32(L1)
            ptil = arena.take_f32(L1)
            lp = arena.take_f32(L1)
            mxb = arena.take_f32(CHB)
            cab = arena.take_f32(CHB)
            cbuf = arena.take_f32(L1)
            cdb = arena.take_f32(CHB)
            btb = arena.take_f32(L1)
            ntil = arena.take_f32(L1)
            Eb = arena.take_f32(CHB)
            denb = arena.take_f32(CHB)
            rdnb = arena.take_f32(CHB)
            xa_g = lambda g: xa_pad[:, g * L1:(g + 1) * L1]

            wait_halves("gpsimd", (0, 1))
            for half in range(2):
                bld.op("gpsimd", lambda E, half=half: E.dma_start(
                    out=w_ring[:, half * RING_H:(half + 1) * RING_H].rearrange(
                        "p (k m) -> p k m", k=4),
                    in_=wkvr.rearrange("(kk k p) m -> kk p k m", kk=2, p=128)[half]),
                    inc=WD, n=16)
            wd_done = WD.n
            if c5 == 0:
                bld.op("gpsimd", lambda E: E.memset(cbuf[:, 0:1], 0.0))
                bld.op("gpsimd", lambda E: E.memset(ewt[:, 0:1], 0.0), inc=GP)
                bld.dwait("vector", GP, GP.n)

            for sub in range(2):
                c0 = c5 * HT + sub * CHB
                lnb = attb_ln_rows(c0, CHB, xpb, xsb, tmp_f, True, want_r1=True)
                bank_s = lnb["s"][1]
                bank_m = lnb["m"][1]
                bld.wait("vector", PE, PE.n)
                for g in range(G):
                    if c5 == 0 and sub == 0:
                        bld.op("vector", lambda E, g=g: E.memset(xa_g(g)[:, 0:1], 0.0))
                    else:
                        bld.op("vector", lambda E, g=g: E.tensor_copy(
                            xa_g(g)[:, 0:1], xa_g(g)[:, CHB:CHB + 1]))
                for g in range(G):
                    bld.op("vector", lambda E, g=g, bank_s=bank_s, c0=c0: E.tensor_mul(
                        tmp_f, x_fm[:, g * T + c0: g * T + c0 + CHB], bank_s[:, 0:CHB]))
                    bld.op("vector", lambda E, g=g, bank_m=bank_m: E.tensor_add(
                        tmp_f, tmp_f, bank_m[:, 0:CHB]))
                    bld.op("vector", lambda E, g=g: E.tensor_scalar(
                        out=xa_g(g)[:, 1:], in0=tmp_f, scalar1=vcol("ln1g", g),
                        scalar2=vcol("ln1b", g), op0=ALU.mult, op1=ALU.add),
                        inc=(V if g == G - 1 else None))
                banks.release(lnb["s"][0], V, V.n)
                banks.release(lnb["m"][0], V, V.n)
                for g in range(G):
                    bld.op("vector", lambda E, g=g: E.tensor_sub(
                        dbuf, xa_g(g)[:, 1:], xa_g(g)[:, 0:CHB]))
                    for name, buf in (("rk", xkb), ("rv", xvb), ("rr", xrb)):
                        bld.op("vector", lambda E, g=g, name=name, buf=buf: E.scalar_tensor_tensor(
                            out=buf[:, g * CHB:(g + 1) * CHB], in0=dbuf, scalar=vcol(name, g),
                            in1=xa_g(g)[:, 0:CHB], op0=ALU.mult, op1=ALU.add),
                            inc=(V if g == G - 1 and name == "rr" else None))
                v_mix = V.n
                bld.wait("tensor", WD, wd_done)
                pe_guard()
                bld.wait("tensor", V, v_mix)
                for pi, (mvb, ebuf) in enumerate(((xkb, kbuf), (xvb, vbuf), (xrb, sgbuf))):
                    for m in range(G):
                        bki, bank_k = banks.acquire()
                        for k in range(G):
                            kk, kr = k // 4, k % 4
                            bld.op("tensor", lambda E, k=k, kk=kk, kr=kr, pi=pi, m=m, mvb=mvb, bank_k=bank_k: E.matmul(
                                bank_k[:, 0:CHB],
                                w_ring[:, kk * RING_H + kr * 3072 + pi * 1024 + m * P:
                                       kk * RING_H + kr * 3072 + pi * 1024 + (m + 1) * P],
                                mvb[:, k * CHB:(k + 1) * CHB],
                                start=(k == 0), stop=(k == G - 1)),
                                inc=(PE if k == G - 1 else None))
                        pe_k = PE.n
                        if pi == 0:
                            bld.wait("vector", PE, pe_k)
                            v_e = bld.op("vector", lambda E, m=m, ebuf=ebuf, bank_k=bank_k: E.tensor_copy(
                                ebuf[:, m * CHB:(m + 1) * CHB], bank_k[:, 0:CHB]), inc=V)
                            banks.release(bki, V, v_e)
                        elif pi == 1:
                            bld.wait("vector", PE, pe_k)
                            v_e = bld.op("vector", lambda E, m=m, ebuf=ebuf, bank_k=bank_k: E.tensor_copy(
                                ebuf[:, m * CHB:(m + 1) * CHB], bank_k[:, 0:CHB]), inc=V)
                            banks.release(bki, V, v_e)
                        else:
                            bld.wait("scalar", PE, pe_k)
                            s_e = bld.op("scalar", lambda E, m=m, ebuf=ebuf, bank_k=bank_k: E.activation(
                                out=ebuf[:, m * CHB:(m + 1) * CHB], in_=bank_k[:, 0:CHB],
                                func=ACT.Sigmoid), inc=S)
                            banks.release(bki, S, s_e)
                bank_r1 = lnb["r1"][1]
                bank_mr1 = lnb["mr1"][1]
                bld.dwait("vector", S, S.n)
                for g in range(G):
                    bld.op("vector", lambda E, g=g: E.reduce_max(
                        out=kmaxc[:, g:g + 1], in_=kbuf[:, g * CHB:(g + 1) * CHB], axis=AXX))
                for g in range(G):
                    bld.op("vector", lambda E, g=g: E.tensor_max(
                        mcol[:, g:g + 1], kmaxc[:, g:g + 1], pp_st[:, g:g + 1]))
                for g in range(G):
                    bld.op("vector", lambda E, g=g: E.tensor_sub(
                        argc[:, g:g + 1], pp_st[:, g:g + 1], mcol[:, g:g + 1]))
                for g in range(G):
                    bld.op("vector", lambda E, g=g: E.tensor_scalar_mul(
                        negm[:, g:g + 1], mcol[:, g:g + 1], -1.0), inc=(V if g == G - 1 else None))
                for g in range(G):
                    kslice = kbuf[:, g * CHB:(g + 1) * CHB]
                    vslice = vbuf[:, g * CHB:(g + 1) * CHB]
                    v_km = bld.op("vector", lambda E, g=g, kslice=kslice: E.tensor_single_scalar(
                        out=kmb, in_=kslice, scalar=negm[:, g:g + 1], op=ALU.add), inc=V)
                    bld.op("vector", lambda E, g=g: E.tensor_scalar(
                        out=ewt[:, 1:], in0=kmb, scalar1=0.0, scalar2=vcol("ew", g),
                        op0=ALU.mult, op1=ALU.add))
                    bld.dwait("scalar", V, v_km)
                    bld.op("scalar", lambda E, g=g: E.activation(
                        out=d1b[:, 1:], in_=kmb, func=ACT.Exp, bias=vcol("w", g)))
                    bld.op("scalar", lambda E, g=g: E.activation(
                        out=Eb, in_=kmb, func=ACT.Exp, bias=vcol("u", g)))
                    s_prep = bld.op("scalar", lambda E, g=g: E.activation(
                        out=d1b[:, 0:1], in_=argc[:, g:g + 1], func=ACT.Exp), inc=S)
                    bld.dwait("vector", S, s_prep)
                    bld.op("vector", lambda E: E.tensor_tensor_scan(
                        out=ptil, data0=ewt, data1=d1b, initial=0.0,
                        op0=ALU.mult, op1=ALU.add))
                    bld.op("vector", lambda E: E.tensor_add(denb, ptil[:, 0:CHB], Eb))
                    v_scan = bld.op("vector", lambda E, g=g: E.tensor_mul(
                        btb[:, 0:1], aa_st[:, g:g + 1], d1b[:, 0:1]), inc=V)
                    bld.dwait("scalar", V, v_scan)
                    bld.op("scalar", lambda E: E.activation(out=lp, in_=ptil, func=ACT.Ln))
                    bld.op("scalar", lambda E, g=g: E.activation(
                        out=rpc[:, g:g + 1], in_=lp[:, CHB:CHB + 1], func=ACT.Exp, scale=-1.0))
                    bld.op("scalar", lambda E: E.activation(out=cdb, in_=denb, func=ACT.Ln))
                    s_lp = bld.op("scalar", lambda E: E.activation(
                        out=rdnb, in_=cdb, func=ACT.Exp, scale=-1.0), inc=S)
                    bld.dwait("vector", S, s_lp)
                    bld.op("vector", lambda E: E.tensor_max(mxb, lp[:, 0:CHB], kmb))
                    bld.op("vector", lambda E: E.tensor_sub(cab, lp[:, 1:], mxb))
                    v_c2 = bld.op("vector", lambda E: E.tensor_add(mxb, cab, kmb), inc=V)
                    bld.dwait("scalar", V, v_c2)
                    bld.op("scalar", lambda E: E.activation(out=cbuf[:, 1:], in_=cab, func=ACT.Exp))
                    s_c = bld.op("scalar", lambda E: E.activation(out=cdb, in_=mxb, func=ACT.Exp), inc=S)
                    bld.dwait("vector", S, s_c)
                    bld.op("vector", lambda E, vslice=vslice: E.tensor_mul(btb[:, 1:], cdb, vslice))
                    bld.op("vector", lambda E: E.tensor_tensor_scan(
                        out=ntil, data0=cbuf, data1=btb, initial=0.0,
                        op0=ALU.mult, op1=ALU.add))
                    bld.op("vector", lambda E, vslice=vslice: E.tensor_mul(Eb, Eb, vslice))
                    bld.op("vector", lambda E: E.tensor_add(Eb, ntil[:, 0:CHB], Eb))
                    bld.op("vector", lambda E: E.tensor_mul(Eb, Eb, rdnb))
                    bld.op("vector", lambda E, g=g: E.tensor_mul(
                        Eb, Eb, sgbuf[:, g * CHB:(g + 1) * CHB]))
                    bld.op("vector", lambda E, g=g, c0=c0, bank_r1=bank_r1: E.tensor_mul(
                        cab, x_fm[:, g * T + c0: g * T + c0 + CHB], bank_r1[:, 0:CHB]))
                    bld.op("vector", lambda E, bank_mr1=bank_mr1: E.tensor_add(
                        cab, cab, bank_mr1[:, 0:CHB]))
                    bld.op("vector", lambda E, g=g, sub=sub: E.tensor_add(
                        x1_bf[:, g * HT + sub * CHB: g * HT + sub * CHB + CHB], cab, Eb))
                    bld.op("vector", lambda E, g=g: E.tensor_add(
                        pp_st[:, g:g + 1], mcol[:, g:g + 1], lp[:, CHB:CHB + 1]))
                    bld.op("vector", lambda E, g=g: E.tensor_mul(
                        aa_st[:, g:g + 1], ntil[:, CHB:CHB + 1], rpc[:, g:g + 1]), inc=V)
                banks.release(lnb["r1"][0], V, V.n)
                banks.release(lnb["mr1"][0], V, V.n)
            mark_wring_free()
            arena.pop()
            arena.push()
            xck_bf = arena.take_bf16(G * HT)
            xcr_bf = arena.take_bf16(G * HT)
            sigrc = arena.take_f32(G * HT)
            arena.push()
            xb_pad = arena.take_f32(G * (HT + 1))
            xsb2 = arena.take_bf16(G * HT)
            tmp_h = arena.take_f32(HT)
            xb_g = lambda g: xb_pad[:, g * (HT + 1):(g + 1) * (HT + 1)]
            x1_g = lambda g: x1_bf[:, g * HT:(g + 1) * HT]

            wd_rc = wload_half(0, [(0, 8192, wcr.rearrange("(k p) m -> p k m", p=128))])

            def hg_loads(hg):
                return [(0, 4096, wck[:, hg * 512:(hg + 1) * 512].rearrange(
                            "(k p) m -> p k m", p=128)),
                        (4096, 4096, wcv[hg * 512:(hg + 1) * 512, :].rearrange(
                            "(k p) m -> p k m", p=128))]

            wd_hg = [wload_half(1, hg_loads(0))]
            lnb2 = attb_ln_rows(0, HT, None, xsb2, tmp_h, False, src_fn=x1_g)
            bank_s2 = lnb2["s"][1]
            bank_m2 = lnb2["m"][1]
            bld.wait("vector", PE, PE.n)
            for g in range(G):
                if c5 == 0:
                    bld.op("vector", lambda E, g=g: E.memset(xb_g(g)[:, 0:1], 0.0))
                else:
                    bld.op("vector", lambda E, g=g: E.tensor_copy(
                        xb_g(g)[:, 0:1], state_sb[:, 56 + g:57 + g]))
            for g in range(G):
                bld.op("vector", lambda E, g=g, bank_s2=bank_s2: E.tensor_mul(
                    tmp_h, x1_g(g), bank_s2[:, 0:HT]))
                bld.op("vector", lambda E, g=g, bank_m2=bank_m2: E.tensor_add(
                    tmp_h, tmp_h, bank_m2[:, 0:HT]))
                bld.op("vector", lambda E, g=g: E.tensor_scalar(
                    out=xb_g(g)[:, 1:], in0=tmp_h, scalar1=vcol("ln2g", g),
                    scalar2=vcol("ln2b", g), op0=ALU.mult, op1=ALU.add),
                    inc=(V if g == G - 1 else None))
            banks.release(lnb2["s"][0], V, V.n)
            banks.release(lnb2["m"][0], V, V.n)
            for g in range(G):
                bld.op("vector", lambda E, g=g: E.tensor_copy(
                    state_sb[:, 56 + g:57 + g], xb_g(g)[:, HT:HT + 1]))
            for g in range(G):
                bld.op("vector", lambda E, g=g: E.tensor_sub(
                    tmp_h, xb_g(g)[:, 1:], xb_g(g)[:, 0:HT]))
                bld.op("vector", lambda E, g=g: E.scalar_tensor_tensor(
                    out=xck_bf[:, g * HT:(g + 1) * HT], in0=tmp_h, scalar=vcol("crk", g),
                    in1=xb_g(g)[:, 0:HT], op0=ALU.mult, op1=ALU.add))
                bld.op("vector", lambda E, g=g: E.scalar_tensor_tensor(
                    out=xcr_bf[:, g * HT:(g + 1) * HT], in0=tmp_h, scalar=vcol("crr", g),
                    in1=xb_g(g)[:, 0:HT], op0=ALU.mult, op1=ALU.add),
                    inc=(V if g == G - 1 else None))
            v_mix2 = V.n
            bld.wait("tensor", WD, wd_rc)
            pe_guard()
            bld.wait("tensor", V, v_mix2)
            for m in range(G):
                bri, bank_r = banks.acquire()
                for k in range(G):
                    bld.op("tensor", lambda E, k=k, m=m, bank_r=bank_r: E.matmul(
                        bank_r[:, 0:HT], w_ring[:, k * 1024 + m * P: k * 1024 + (m + 1) * P],
                        xcr_bf[:, k * HT:(k + 1) * HT], start=(k == 0), stop=(k == G - 1)),
                        inc=(PE if k == G - 1 else None))
                pe_rc = PE.n
                bld.wait("scalar", PE, pe_rc)
                s_rc = bld.op("scalar", lambda E, m=m, bank_r=bank_r: E.activation(
                    out=sigrc[:, m * HT:(m + 1) * HT], in_=bank_r[:, 0:HT],
                    func=ACT.Sigmoid), inc=S)
                banks.release(bri, S, s_rc)
            wring_half[0] = (PE, PE.n)
            wd_hg.append(wload_half(0, hg_loads(1)))
            arena.pop()
            arena.push()
            kc_bf = arena.take_bf16(4 * HT)
            tmp2 = arena.take_f32(HT)
            vc = arena.take_f32(G * HT)
            for hg in range(8):
                half = (hg + 1) % 2
                base = half * RING_H
                bld.wait("tensor", WD, wd_hg[hg])
                pe_guard()
                for mh in range(4):
                    bki, bank_k = banks.acquire()
                    for k in range(G):
                        bld.op("tensor", lambda E, k=k, mh=mh, base=base, bank_k=bank_k: E.matmul(
                            bank_k[:, 0:HT],
                            w_ring[:, base + k * 512 + mh * P: base + k * 512 + (mh + 1) * P],
                            xck_bf[:, k * HT:(k + 1) * HT], start=(k == 0), stop=(k == G - 1)),
                            inc=(PE if k == G - 1 else None))
                    pe_kc = PE.n
                    bld.wait("scalar", PE, pe_kc)
                    bld.op("scalar", lambda E, bank_k=bank_k: E.activation(
                        out=tmp2, in_=bank_k[:, 0:HT], func=ACT.Relu))
                    s_kc = bld.op("scalar", lambda E, mh=mh: E.activation(
                        out=kc_bf[:, mh * HT:(mh + 1) * HT], in_=tmp2, func=ACT.Square), inc=S)
                    banks.release(bki, S, s_kc)
                s_kcd = S.n
                bld.wait("tensor", S, s_kcd)
                for o in range(G):
                    bvi, bank_v = banks.acquire()
                    for k2 in range(4):
                        bld.op("tensor", lambda E, k2=k2, o=o, base=base, bank_v=bank_v: E.matmul(
                            bank_v[:, 0:HT],
                            w_ring[:, base + 4096 + k2 * 1024 + o * P: base + 4096 + k2 * 1024 + (o + 1) * P],
                            kc_bf[:, k2 * HT:(k2 + 1) * HT], start=(k2 == 0), stop=(k2 == 3)),
                            inc=(PE if k2 == 3 else None))
                    pe_vc = PE.n
                    bld.wait("vector", PE, pe_vc)
                    if hg == 0:
                        v_vc = bld.op("vector", lambda E, o=o, bank_v=bank_v: E.tensor_copy(
                            vc[:, o * HT:(o + 1) * HT], bank_v[:, 0:HT]), inc=V)
                    else:
                        v_vc = bld.op("vector", lambda E, o=o, bank_v=bank_v: E.tensor_add(
                            vc[:, o * HT:(o + 1) * HT], vc[:, o * HT:(o + 1) * HT],
                            bank_v[:, 0:HT]), inc=V)
                    banks.release(bvi, V, v_vc)
                wring_half[half] = (PE, PE.n)
                if hg + 2 < 8:
                    wd_hg.append(wload_half((hg + 2 + 1) % 2, hg_loads(hg + 2)))
            bld.dwait("vector", S, S.n)
            for g in range(G):
                bld.op("vector", lambda E, g=g: E.tensor_mul(
                    vc[:, g * HT:(g + 1) * HT], vc[:, g * HT:(g + 1) * HT],
                    sigrc[:, g * HT:(g + 1) * HT]))
                bld.op("vector", lambda E, g=g, c5=c5: E.tensor_add(
                    x_fm[:, g * T + c5 * HT: g * T + (c5 + 1) * HT],
                    x_fm[:, g * T + c5 * HT: g * T + (c5 + 1) * HT],
                    vc[:, g * HT:(g + 1) * HT]))
                bld.op("vector", lambda E, g=g, c5=c5: E.tensor_add(
                    x_fm[:, g * T + c5 * HT: g * T + (c5 + 1) * HT],
                    x_fm[:, g * T + c5 * HT: g * T + (c5 + 1) * HT],
                    x1_g(g)), inc=V)
            mark_wring_free()
            arena.pop()
            arena.pop()
        arena.pop()

    for s in range(SEQ_PER_CORE):
        load_seq(s)
        if "attn_a" in phases:
            phase_attn_a(s)
        if "ffn1" in phases:
            phase_ffn(s)
        if "attnb" in phases:
            phase_attnb(s)
        if "ffn2" in phases:
            phase_ffn(s)
        store_seq(s)

    bld.wait("sync", OD, OD.n)

    with nc.Block() as block:
        bld.replay(block)
    return nc


def prep_inputs(inputs):
    def c(a):
        return np.ascontiguousarray(a, dtype=np.float32)

    Wz = inputs["L0_attn_a_Wz"]
    Wh = inputs["L0_attn_a_Wh"]
    wzh = c(np.concatenate([np.transpose(Wz, (1, 0, 2)).reshape(D, D),
                            np.transpose(Wh, (1, 0, 2)).reshape(D, D)], axis=1))
    wg_a = c(inputs["L0_attn_a_Wg"])

    def packvec(v):
        return np.asarray(v, np.float32).reshape(8, 128).T

    names = ["bz", "bh", "bg", "w", "u", "rk", "rv", "rr", "crk", "crr",
             "ln1g", "ln1b", "ln2g", "ln2b", "ew"]
    src = {
        "bz": np.asarray(inputs["L0_attn_a_bz"]).reshape(-1),
        "bh": np.asarray(inputs["L0_attn_a_bh"]).reshape(-1),
        "bg": inputs["L0_attn_a_bg"],
        "w": inputs["L0_attn_b_w"], "u": inputs["L0_attn_b_u"],
        "rk": inputs["L0_attn_b_rk"], "rv": inputs["L0_attn_b_rv"],
        "rr": inputs["L0_attn_b_rr"], "crk": inputs["L0_attn_b_crk"],
        "crr": inputs["L0_attn_b_crr"],
        "ln1g": inputs["L0_attn_b_ln1_g"], "ln1b": inputs["L0_attn_b_ln1_b"],
        "ln2g": inputs["L0_attn_b_ln2_g"], "ln2b": inputs["L0_attn_b_ln2_b"],
        "ew": np.zeros(D, np.float32),
    }
    vecs = np.zeros((P, 8 * 16), np.float32)
    for i, n in enumerate(names):
        vecs[:, 8 * i:8 * i + 8] = packvec(src[n])

    wgu_r = c(np.concatenate([inputs["L0_ffn_wg_r"], inputs["L0_ffn_wu_r"]], axis=2))
    wd_r = c(inputs["L0_ffn_wd_r"])
    wgu_s = c(np.concatenate([inputs["L0_ffn_wg_s"], inputs["L0_ffn_wu_s"]], axis=2))
    wd_s = c(inputs["L0_ffn_wd_s"])
    wkvr = c(np.concatenate([np.asarray(inputs["L0_attn_b_Wk"]).T,
                             np.asarray(inputs["L0_attn_b_Wv"]).T,
                             np.asarray(inputs["L0_attn_b_Wr"]).T], axis=1))
    wck = c(np.asarray(inputs["L0_attn_b_Wck"]).T)
    wcv = c(np.asarray(inputs["L0_attn_b_Wcv"]).T)
    wcr = c(np.asarray(inputs["L0_attn_b_Wcr"]).T)

    onehot = np.zeros((16, 2048), np.float32)
    for e in range(16):
        onehot[e, e * 128:(e + 1) * 128] = 1.0

    shared = dict(
        wzh=wzh, wg_a=wg_a, vecs=vecs, idn=np.eye(P, dtype=np.float32),
        rw=c(inputs["L0_ffn_router_w"]), wgu_r=wgu_r, wd_r=wd_r,
        wgu_s=wgu_s, wd_s=wd_s, wkvr=wkvr, wck=wck, wcv=wcv, wcr=wcr,
        onehot=onehot,
    )
    x = np.asarray(inputs["L0_attn_a_x"], np.float32)
    shards = np.split(np.ascontiguousarray(x), N_CORES, axis=0)
    in_maps = [dict(shared, x=np.ascontiguousarray(sh)) for sh in shards]
    return in_maps


def kernel(**inputs):
    in_maps = prep_inputs(inputs)
    nc = build_program()
    res = run_bass_kernel_spmd(nc, in_maps, core_ids=list(range(N_CORES)))
    return np.concatenate([r["y"] for r in res.results], axis=0)


if __name__ == "__main__":
    pass


# version 02
# speedup vs baseline: 1.0513x; 1.0513x over previous
import sys
import numpy as np

sys.path.insert(0, "/opt/trn_rl_repo")

import concourse.bass as bass
import concourse.mybir as mybir
from concourse.bass_utils import run_bass_kernel_spmd

F32 = mybir.dt.float32
BF16 = mybir.dt.bfloat16
ALU = mybir.AluOpType
ACT = mybir.ActivationFunctionType
AXX = mybir.AxisListType.X

P = 128
D = 1024
G = 8
T = 2048
CH = 512
NCHUNK = T // CH
SEQ_PER_CORE = 2
N_CORES = 8
LN_EPS = 1e-5

PHASES = ("attn_a", "ffn1", "attnb", "ffn2")
DEBUG_TAG = None

ENGINES = ("tensor", "vector", "scalar", "gpsimd", "sync")


class Sem:
    def __init__(self, handle):
        self.h = handle
        self.n = 0


class Builder:

    def __init__(self, nc):
        self.nc = nc
        self.q = {e: [] for e in ENGINES}
        self.sems = {}

    def sem(self, name):
        s = Sem(self.nc.semaphore(name).__enter__())
        self.sems[name] = s
        return s

    def emit(self, eng, fn):
        self.q[eng].append(fn)

    def wait(self, eng, sem, val=None):
        v = sem.n if val is None else val
        if v <= 0:
            return
        self.emit(eng, lambda E, h=sem.h, v=v: E.wait_ge(h, v))

    def dwait(self, eng, sem, val=None, n=3):
        self.wait(eng, sem, val)
        sp = self.spacer_ap
        for _ in range(n):
            if eng == "scalar":
                self.emit(eng, lambda E, sp=sp: E.copy(sp[1], sp[0]))
            else:
                self.emit(eng, lambda E, sp=sp: E.tensor_copy(sp[1], sp[0]))

    def op(self, eng, fn, inc=None, n=1):
        if inc is None:
            self.emit(eng, fn)
            return None
        inc.n += n
        self.emit(eng, lambda E, fn=fn, h=inc.h, n=n: fn(E).then_inc(h, n))
        return inc.n

    def replay(self, block):
        for name in ENGINES:
            section = getattr(block, name)
            q = self.q[name]

            def runner(E, q=q):
                for fn in q:
                    fn(E)

            section(runner)


class Arena:

    def __init__(self, tensor, ncols):
        self.t = tensor
        self.ncols = ncols
        self.off = 0
        self.marks = []

    def take_f32(self, ncols):
        alloc = (ncols + 15) // 16 * 16
        assert self.off + alloc <= self.ncols, f"arena overflow: {self.off}+{alloc} > {self.ncols}"
        ap = self.t[:, self.off:self.off + ncols]
        self.off += alloc
        return ap

    def take_bf16(self, ncols):
        f32cols = (ncols + 1) // 2
        return self.take_f32(f32cols).bitcast(BF16)

    def push(self):
        self.marks.append(self.off)

    def pop(self):
        self.off = self.marks.pop()


class BankPool:

    def __init__(self, bld, banks):
        self.bld = bld
        self.banks = banks
        self.free_at = [None] * len(banks)
        self.rr = 0

    def acquire(self, eng="tensor"):
        for _ in range(len(self.banks)):
            i = self.rr % len(self.banks)
            self.rr += 1
            if not getattr(self, "in_use", None):
                self.in_use = [False] * len(self.banks)
            if self.in_use[i]:
                continue
            fa = self.free_at[i]
            if fa is not None:
                self.bld.wait(eng, fa[0], fa[1])
            self.free_at[i] = None
            self.in_use[i] = True
            return i, self.banks[i]
        raise AssertionError("all PSUM banks in use")

    def release(self, i, sem, val):
        self.free_at[i] = (sem, val)
        self.in_use[i] = False


def build_program(phases=None, debug_dump=None):
    phases = PHASES if phases is None else phases
    nc = bass.Bass("TRN2", target_bir_lowering=False, debug=False)

    x_in = nc.dram_tensor("x", [SEQ_PER_CORE, T, D], F32, kind="ExternalInput").ap()
    y_out = nc.dram_tensor("y", [SEQ_PER_CORE, T, D], F32, kind="ExternalOutput").ap()
    dbg_out = None
    if DEBUG_TAG is not None:
        dbg_out = nc.dram_tensor("dbg", [P, 8192], F32, kind="ExternalOutput").ap()

    def din(name, shape):
        return nc.dram_tensor(name, shape, F32, kind="ExternalInput").ap()

    wzh = din("wzh", [D, 2 * D])
    wg_a = din("wg_a", [D, D])
    vecs = din("vecs", [P, 8 * 16])
    idn = din("idn", [P, P])
    rw = din("rw", [D, 16])
    wgu_r = din("wgu_r", [16, D, 512])
    wd_r = din("wd_r", [16, 256, D])
    wgu_s = din("wgu_s", [2, D, 1024])
    wd_s = din("wd_s", [2, 512, D])
    onehot = din("onehot", [16, 2048])
    wkvr = din("wkvr", [D, 3 * D])
    wck = din("wck", [D, 4096])
    wcv = din("wcv", [4096, D])
    wcr = din("wcr", [D, D])

    bld = Builder(nc)
    WD = bld.sem("WD")
    XD = bld.sem("XD")
    OD = bld.sem("OD")
    PE = bld.sem("PE")
    V = bld.sem("V")
    S = bld.sem("S")
    GP = bld.sem("GP")

    x_fm = nc.alloc_sbuf_tensor("x_fm", [P, G * T], F32)
    w_ring = nc.alloc_sbuf_tensor("w_ring", [P, 24576], BF16)
    arena_t = nc.alloc_sbuf_tensor("arena", [P, 17408], F32)
    io_st = nc.alloc_sbuf_tensor("io_st", [P, 2 * D], F32)
    rows = nc.alloc_sbuf_tensor("rows", [P, 3072], F32)
    vec_sb = nc.alloc_sbuf_tensor("vec_sb", [P, 8 * 16 + 32], F32)
    idn_sb = nc.alloc_sbuf_tensor("idn_sb", [P, P], F32)
    ones_bf = nc.alloc_sbuf_tensor("ones_bf", [P, 2], BF16)
    ones_row = nc.alloc_sbuf_tensor("ones_row", [1, P], F32)
    spacer_sb = nc.alloc_sbuf_tensor("spacer_sb", [P, 64], F32)
    bld.spacer_ap = (spacer_sb[:, 0:16], spacer_sb[:, 16:32])
    rw_sb = nc.alloc_sbuf_tensor("rw_sb", [P, 128], BF16)
    onehot_sb = nc.alloc_sbuf_tensor("onehot_sb", [P, 2048], BF16)
    state_sb = nc.alloc_sbuf_tensor("state_sb", [P, 64], F32)

    psum = [nc.alloc_psum_tensor(f"ps{i}", [P, CH], F32) for i in range(8)]
    banks = BankPool(bld, psum)

    arena = Arena(arena_t, 17408)

    VEC_IDX = {n: i for i, n in enumerate(
        ["bz", "bh", "bg", "w", "u", "rk", "rv", "rr", "crk", "crr",
         "ln1g", "ln1b", "ln2g", "ln2b", "ew"])}

    def vcol(name, g):
        return vec_sb[:, VEC_IDX[name] * 8 + g: VEC_IDX[name] * 8 + g + 1]

    bld.op("sync", lambda E: E.dma_start(out=vec_sb[:, 0:128], in_=vecs), inc=XD, n=16)
    bld.op("sync", lambda E: E.dma_start(out=idn_sb[:], in_=idn), inc=XD, n=16)
    bld.op("gpsimd", lambda E: E.dma_start(out=onehot_sb[0:16, :], in_=onehot), inc=WD, n=16)
    eps_col = vec_sb[:, 128:129]
    bld.op("gpsimd", lambda E: E.memset(ones_bf[:], 1.0))
    bld.op("gpsimd", lambda E: E.memset(ones_row[:], 1.0))
    bld.op("gpsimd", lambda E: E.memset(eps_col, LN_EPS))
    bld.op("gpsimd", lambda E: E.memset(state_sb[:], 0.0), inc=GP)
    bld.wait("scalar", XD)
    bld.op("scalar", lambda E: E.activation(
        out=vec_sb[:, VEC_IDX["ew"] * 8:VEC_IDX["ew"] * 8 + 8],
        in_=vec_sb[:, VEC_IDX["w"] * 8:VEC_IDX["w"] * 8 + 8], func=ACT.Exp), inc=S)
    for _e in ENGINES:
        bld.wait(_e, XD)
        bld.wait(_e, GP)
        bld.wait(_e, S)
    for _ in range(4):
        bld.op("vector", lambda E: E.tensor_copy(spacer_sb[:, 16:32], spacer_sb[:, 0:16]))
        bld.op("scalar", lambda E: E.copy(spacer_sb[:, 32:48], spacer_sb[:, 0:16]))

    def load_seq(s):
        for ti in range(T // P):
            buf = io_st[:, (ti % 2) * D:(ti % 2) * D + D]
            bld.wait("sync", PE, PE.n)
            ld = bld.op("sync", lambda E, buf=buf, s=s, ti=ti: E.dma_start(
                out=buf, in_=x_in[s, ti * P:(ti + 1) * P, :]), inc=XD, n=16)
            bld.wait("tensor", XD, ld)
            pe_guard()
            for g in range(G):
                bi, bank = banks.acquire()
                pe_v = bld.op("tensor", lambda E, bank=bank, buf=buf, g=g: E.transpose(
                    bank[:, 0:P], buf[:, g * P:(g + 1) * P], idn_sb[:]), inc=PE)
                bld.wait("vector", PE, pe_v)
                v_v = bld.op("vector", lambda E, bank=bank, g=g, ti=ti: E.tensor_copy(
                    x_fm[:, g * T + ti * P: g * T + (ti + 1) * P], bank[:, 0:P]), inc=V)
                banks.release(bi, V, v_v)

    def store_seq(s):
        bld.wait("tensor", V, V.n)
        for ti in range(T // P):
            buf = io_st[:, (ti % 2) * D:(ti % 2) * D + D]
            if ti >= 2:
                bld.wait("vector", OD, (store_seq.count - 1) * 16)
            for g in range(G):
                bi, bank = banks.acquire()
                pe_v = bld.op("tensor", lambda E, bank=bank, g=g, ti=ti: E.transpose(
                    bank[:, 0:P], x_fm[:, g * T + ti * P: g * T + (ti + 1) * P], idn_sb[:]),
                    inc=PE)
                bld.wait("vector", PE, pe_v)
                v_v = bld.op("vector", lambda E, bank=bank, buf=buf, g=g: E.tensor_copy(
                    buf[:, g * P:(g + 1) * P], bank[:, 0:P]), inc=V)
                banks.release(bi, V, v_v)
            bld.wait("sync", V, V.n)
            bld.op("sync", lambda E, buf=buf, s=s, ti=ti: E.dma_start(
                out=y_out[s, ti * P:(ti + 1) * P, :], in_=buf), inc=OD, n=16)
            store_seq.count += 1
    store_seq.count = 0

    def pe_guard(n=8):
        for _ in range(n):
            bld.op("tensor", lambda E: E.ldweights(ones_bf[:, 0:1]))

    dbg_state = {"done": False}

    def dbg_dump(tag, ap_f32, cols):
        if DEBUG_TAG != tag or dbg_state["done"]:
            return
        dbg_state["done"] = True
        bld.wait("sync", V, V.n)
        bld.wait("sync", S, S.n)
        bld.op("sync", lambda E: E.dma_start(out=dbg_out[:, 0:cols], in_=ap_f32), inc=OD, n=16)
        for _e in ("vector", "scalar", "tensor", "gpsimd"):
            bld.wait(_e, OD, OD.n)

    wring_half = [None, None]

    def wait_halves(eng, halves):
        for h in halves:
            if wring_half[h] is not None:
                bld.wait(eng, wring_half[h][0], wring_half[h][1])
                wring_half[h] = None

    def wload(dst_cols, src_ap):
        wait_halves("gpsimd", (0, 1))
        return bld.op("gpsimd", lambda E, d=dst_cols, s=src_ap: E.dma_start(
            out=w_ring[:, d[0]:d[1]], in_=s), inc=WD, n=16)

    def mark_wring_free():
        wring_half[0] = (PE, PE.n)
        wring_half[1] = (PE, PE.n)

    def ln_cast(s, c, xbf_b, tmp_f, dest_fn):
        xc = lambda g: x_fm[:, g * T + c * CH: g * T + (c + 1) * CH]
        for g in range(G):
            bld.op("vector", lambda E, g=g: E.tensor_copy(
                dest_fn(g), xc(g)), inc=(V if g == G - 1 else None))
        bld.dwait("scalar", V, V.n)
        for g in range(G):
            bld.op("scalar", lambda E, g=g: E.activation(
                out=xbf_b[:, g * CH:(g + 1) * CH], in_=xc(g), func=ACT.Square),
                inc=(S if g == G - 1 else None))
        b1i, b1 = banks.acquire()
        bld.wait("tensor", V, V.n)
        for g in range(G):
            bld.op("tensor", lambda E, g=g, b1=b1: E.matmul(
                b1[0:1, 0:CH], ones_bf[:, 0:1], dest_fn(g),
                start=(g == 0), stop=(g == G - 1)), inc=(PE if g == G - 1 else None))
        b2i, b2 = banks.acquire()
        bld.wait("tensor", S, S.n)
        for g in range(G):
            bld.op("tensor", lambda E, g=g, b2=b2: E.matmul(
                b2[0:1, 0:CH], ones_bf[:, 0:1], xbf_b[:, g * CH:(g + 1) * CH],
                start=(g == 0), stop=(g == G - 1)), inc=(PE if g == G - 1 else None))
        mu_row = rows[0:1, 0:CH]
        r_row = rows[0:1, CH:2 * CH]
        t_row = rows[0:1, 2 * CH:3 * CH]
        nm_row = rows[0:1, 3 * CH:4 * CH]
        bld.wait("vector", PE, PE.n)
        bld.op("vector", lambda E: E.tensor_scalar_mul(mu_row, b1[0:1, 0:CH], 1.0 / D))
        v1 = bld.op("vector", lambda E: E.tensor_scalar_mul(t_row, b2[0:1, 0:CH], 1.0 / D), inc=V)
        banks.release(b1i, V, v1)
        banks.release(b2i, V, v1)
        bld.op("vector", lambda E: E.tensor_mul(nm_row, mu_row, mu_row))
        bld.op("vector", lambda E: E.tensor_sub(t_row, t_row, nm_row), inc=V)
        bld.wait("scalar", V, V.n)
        bld.op("scalar", lambda E: E.activation(out=t_row, in_=t_row, func=ACT.Ln, bias=eps_col[0:1, :]))
        bld.op("scalar", lambda E: E.activation(out=r_row, in_=t_row, func=ACT.Exp, scale=-0.5), inc=S)
        bld.dwait("vector", S, S.n)
        bld.op("vector", lambda E: E.scalar_tensor_tensor(
            out=nm_row, in0=mu_row, scalar=-1.0, in1=r_row, op0=ALU.mult, op1=ALU.mult), inc=V)
        bri, bank_r = banks.acquire()
        bld.wait("tensor", V, V.n)
        bld.op("tensor", lambda E, bank_r=bank_r: E.matmul(
            bank_r[:, 0:CH], ones_row[0:1, :], r_row, start=True, stop=True))
        bmi, bank_m = banks.acquire()
        bld.op("tensor", lambda E, bank_m=bank_m: E.matmul(
            bank_m[:, 0:CH], ones_row[0:1, :], nm_row, start=True, stop=True), inc=PE)
        bld.wait("vector", PE, PE.n)
        for g in range(G):
            bld.op("vector", lambda E, g=g, bank_r=bank_r: E.tensor_mul(
                tmp_f, xc(g), bank_r[:, 0:CH]))
            bld.op("vector", lambda E, g=g, bank_m=bank_m: E.tensor_add(
                dest_fn(g), tmp_f, bank_m[:, 0:CH]),
                inc=(V if g == G - 1 else None))
        banks.release(bri, V, V.n)
        banks.release(bmi, V, V.n)
        return r_row, mu_row, V.n

    def phase_attn_a(s):
        arena.push()
        CH1 = CH + 1
        xbf_a = arena.take_bf16(G * CH)
        xbf_b = arena.take_bf16(G * CH)
        tmp_f = arena.take_f32(CH)
        h_full = arena.take_f32(G * CH1)
        zslot = [arena.take_f32(CH) for _ in range(2)]
        aslot = [arena.take_f32(CH1) for _ in range(2)]
        bslot = [arena.take_f32(CH1) for _ in range(2)]
        gslot = [arena.take_f32(CH) for _ in range(2)]
        hbf = arena.take_bf16(G * CH)

        for k in range(G):
            wload((k * 2048, (k + 1) * 2048), wzh[k * P:(k + 1) * P, :])
        for k in range(G):
            wload((16384 + k * 1024, 16384 + (k + 1) * 1024), wg_a[k * P:(k + 1) * P, :])
        wd_done = WD.n

        bld.op("vector", lambda E: E.memset(aslot[0][:, 0:1], 0.0))
        bld.op("vector", lambda E: E.memset(aslot[1][:, 0:1], 0.0), inc=V)

        dbgrow = arena.take_f32(16)
        zuse = [0, 0]
        guse = [0, 0]
        x_bf = xbf_a
        for c in range(NCHUNK):
            _, _, v_xbf = ln_cast(s, c, xbf_b, tmp_f,
                                  dest_fn=lambda g: xbf_a[:, g * CH:(g + 1) * CH])
            bld.wait("tensor", WD, wd_done)
            bld.wait("tensor", V, v_xbf)
            pe_guard()
            for m in range(G):
                zs, as_, bs = zslot[m % 2], aslot[m % 2], bslot[m % 2]
                hsl = h_full[:, m * CH1:(m + 1) * CH1]
                if c == 0:
                    bld.op("vector", lambda E, bs=bs: E.memset(bs[:, 0:1], 0.0))
                else:
                    bld.op("vector", lambda E, bs=bs, hsl=hsl: E.tensor_copy(
                        bs[:, 0:1], hsl[:, CH:CH + 1]))
                bzi, bank_z = banks.acquire()
                for k in range(G):
                    bld.op("tensor", lambda E, k=k, m=m, bank_z=bank_z: E.matmul(
                        bank_z[:, 0:CH], w_ring[:, k * 2048 + m * P: k * 2048 + (m + 1) * P],
                        x_bf[:, k * CH:(k + 1) * CH], start=(k == 0), stop=(k == G - 1)),
                        inc=(PE if k == G - 1 else None))
                pe_z = PE.n
                bhi, bank_h = banks.acquire()
                for k in range(G):
                    bld.op("tensor", lambda E, k=k, m=m, bank_h=bank_h: E.matmul(
                        bank_h[:, 0:CH], w_ring[:, k * 2048 + 1024 + m * P: k * 2048 + 1024 + (m + 1) * P],
                        x_bf[:, k * CH:(k + 1) * CH], start=(k == 0), stop=(k == G - 1)),
                        inc=(PE if k == G - 1 else None))
                pe_h = PE.n
                bld.wait("scalar", PE, pe_z)
                bld.wait("scalar", V, zuse[m % 2])
                s_z = bld.op("scalar", lambda E, zs=zs, m=m, bank_z=bank_z: E.activation(
                    out=zs, in_=bank_z[:, 0:CH], func=ACT.Sigmoid, bias=vcol("bz", m)), inc=S)
                banks.release(bzi, S, s_z)
                bld.dwait("vector", S, s_z)
                bld.op("vector", lambda E, as_=as_, zs=zs: E.tensor_scalar(
                    out=as_[:, 1:], in0=zs, scalar1=-1.0, scalar2=1.0, op0=ALU.mult, op1=ALU.add))
                bld.wait("vector", PE, pe_h)
                v_b = bld.op("vector", lambda E, bs=bs, zs=zs, m=m, bank_h=bank_h: E.scalar_tensor_tensor(
                    out=bs[:, 1:], in0=bank_h[:, 0:CH], scalar=vcol("bh", m), in1=zs,
                    op0=ALU.add, op1=ALU.mult), inc=V)
                banks.release(bhi, V, v_b)
                zuse[m % 2] = v_b
                bld.op("vector", lambda E, as_=as_, bs=bs, hsl=hsl: E.tensor_tensor_scan(
                    out=hsl, data0=as_, data1=bs,
                    initial=0.0, op0=ALU.mult, op1=ALU.add))
                bld.op("vector", lambda E, m=m, hsl=hsl: E.tensor_copy(
                    hbf[:, m * CH:(m + 1) * CH], hsl[:, 1:]),
                    inc=V)
            v_hready = V.n
            bld.wait("tensor", V, v_hready)
            for m in range(G):
                gsl = gslot[m % 2]
                bgi, bank_g = banks.acquire()
                for k in range(G):
                    bld.op("tensor", lambda E, k=k, m=m, bank_g=bank_g: E.matmul(
                        bank_g[:, 0:CH], w_ring[:, 16384 + k * 1024 + m * P: 16384 + k * 1024 + (m + 1) * P],
                        hbf[:, k * CH:(k + 1) * CH], start=(k == 0), stop=(k == G - 1)),
                        inc=(PE if k == G - 1 else None))
                pe_g = PE.n
                bld.wait("scalar", PE, pe_g)
                bld.wait("scalar", V, guse[m % 2])
                s_g = bld.op("scalar", lambda E, gsl=gsl, m=m, bank_g=bank_g: E.activation(
                    out=gsl, in_=bank_g[:, 0:CH], func=ACT.Sigmoid, bias=vcol("bg", m)), inc=S)
                banks.release(bgi, S, s_g)
                bld.dwait("vector", S, s_g)
                bld.op("vector", lambda E, gsl=gsl, m=m: E.tensor_mul(
                    gsl, gsl, h_full[:, m * CH1 + 1:(m + 1) * CH1]))
                guse[m % 2] = bld.op("vector", lambda E, gsl=gsl, m=m, c=c: E.tensor_add(
                    x_fm[:, m * T + c * CH: m * T + (c + 1) * CH],
                    x_fm[:, m * T + c * CH: m * T + (c + 1) * CH], gsl), inc=V)
        mark_wring_free()
        arena.pop()

    RING_H = 12288

    def wload_half(h, loads):
        wait_halves("gpsimd", (h,))
        base = h * RING_H
        for (dst, ncols, src) in loads:
            k = src.shape[1] if src.ndim == 3 else None
            bld.op("gpsimd", lambda E, dst=dst, ncols=ncols, src=src, k=k: E.dma_start(
                out=(w_ring[:, base + dst:base + dst + ncols].rearrange(
                    "p (k m) -> p k m", k=k) if k else w_ring[:, base + dst:base + dst + ncols]),
                in_=src), inc=WD, n=16)
        return WD.n

    def spacer(eng, n=2):
        for _ in range(n):
            if eng == "scalar":
                bld.op(eng, lambda E: E.copy(spacer_sb[:, 32:64], spacer_sb[:, 0:32]))
            else:
                bld.op(eng, lambda E: E.tensor_copy(spacer_sb[:, 32:64], spacer_sb[:, 0:32]))

    def phase_ffn(s):
        arena.push()
        xn_bf = arena.take_bf16(G * T)
        comb_fm = arena.take_bf16(T)
        arena.push()
        xbf_b = arena.take_bf16(G * CH)
        tmp_f = arena.take_f32(CH)
        lt_fm = arena.take_f32(CH)
        lt_tm = arena.take_f32(16 * 16)
        el_all = arena.take_f32(16 * 16)
        el2 = arena.take_f32(16 * 16)
        mx8_all = arena.take_f32(16 * 8)
        r16a = arena.take_f32(16)
        r16b = arena.take_f32(16)
        r16c = arena.take_f32(16)
        r16d = arena.take_f32(16)
        r16e = arena.take_f32(16)
        r16f = arena.take_f32(16)
        r16g = arena.take_f32(16)

        bld.op("gpsimd", lambda E: E.dma_start(
            out=rw_sb[:, 0:128].rearrange("p (k m) -> p k m", k=8),
            in_=rw.rearrange("(k p) m -> p k m", p=128)), inc=WD, n=16)
        wd_rw = WD.n

        for c in range(NCHUNK):
            ln_cast(s, c, xbf_b, tmp_f,
                    dest_fn=lambda g, c=c: xn_bf[:, g * T + c * CH: g * T + (c + 1) * CH])
        v_xn = V.n

        def xn_mv(k, n):
            return xn_bf[:, k * T + n * CH:(k) * T + (n + 1) * CH]

        bld.wait("tensor", WD, wd_rw)
        bld.wait("tensor", V, v_xn)
        pe_guard()
        for n in range(NCHUNK):
            bli, bank_l = banks.acquire()
            for k in range(G):
                bld.op("tensor", lambda E, k=k, n=n, bank_l=bank_l: E.matmul(
                    bank_l[0:16, 0:CH], rw_sb[:, k * 16:(k + 1) * 16], xn_mv(k, n),
                    start=(k == 0), stop=(k == G - 1)), inc=(PE if k == G - 1 else None))
            bld.wait("vector", PE, PE.n)
            v_l = bld.op("vector", lambda E, bank_l=bank_l: E.tensor_copy(
                lt_fm[0:16, 0:CH], bank_l[0:16, 0:CH]), inc=V)
            banks.release(bli, V, v_l)
            bld.wait("tensor", V, v_l)
            pe_guard()
            for tt in range(4):
                bti, bank_t = banks.acquire()
                pe_t = bld.op("tensor", lambda E, tt=tt, bank_t=bank_t: E.transpose(
                    bank_t[0:P, 0:16], lt_fm[0:16, tt * P:(tt + 1) * P],
                    idn_sb[0:16, 0:16]), inc=PE)
                bld.wait("vector", PE, pe_t)
                i = n * 4 + tt
                v_t = bld.op("vector", lambda E, i=i, bank_t=bank_t: E.tensor_copy(
                    lt_tm[:, i * 16:(i + 1) * 16], bank_t[:, 0:16]), inc=V)
                banks.release(bti, V, v_t)
        if s == 0:
            dbg_dump("ffn_lt", lt_tm, 256)
        bld.op("vector", lambda E: E.memset(r16g, 0.0))
        for i in range(16):
            bld.op("vector", lambda E, i=i: E.max(
                out=mx8_all[:, i * 8:(i + 1) * 8], in_=lt_tm[:, i * 16:(i + 1) * 16]))
        spacer("vector", 3)
        bld.op("vector", lambda E: E.tensor_sub(
            r16a, r16g, mx8_all[:, 0:128:8]))
        bld.op("vector", lambda E: E.tensor_sub(
            r16b, mx8_all[:, 1:128:8], mx8_all[:, 0:128:8]), inc=V)
        spacer("vector", 2)
        for i in range(16):
            bld.op("vector", lambda E, i=i: E.tensor_single_scalar(
                out=el_all[:, i * 16:(i + 1) * 16], in_=lt_tm[:, i * 16:(i + 1) * 16],
                scalar=r16a[:, i:i + 1], op=ALU.add), inc=(V if i == 15 else None))
        bld.wait("scalar", V, V.n)
        bld.op("scalar", lambda E: E.activation(out=r16g, in_=r16b, func=ACT.Exp))
        s_el = bld.op("scalar", lambda E: E.activation(out=el2, in_=el_all, func=ACT.Exp), inc=S)
        el_all = el2
        bld.dwait("vector", S, s_el)
        for i in range(16):
            bld.op("vector", lambda E, i=i: E.reduce_sum(
                out=r16c[:, i:i + 1], in_=el_all[:, i * 16:(i + 1) * 16], axis=AXX))
        spacer("vector", 3)
        bld.op("vector", lambda E: E.tensor_scalar(
            out=r16e, in0=r16c, scalar1=1e-9, scalar2=1.0, op0=ALU.mult, op1=ALU.add))
        spacer("vector", 3)
        bld.op("vector", lambda E: E.tensor_add(r16f, r16e, r16g))
        spacer("vector", 3)
        bld.op("vector", lambda E: E.reciprocal(out=r16d, in_=r16f))
        spacer("vector", 3)
        if s == 0 and DEBUG_TAG in ("ffn_rows", "ffn_rows2"):
            bld.op("vector", lambda E: E.tensor_copy(el_all[:, 0:16], r16a))
            bld.op("vector", lambda E: E.tensor_copy(el_all[:, 16:32], r16b))
            bld.op("vector", lambda E: E.tensor_copy(el_all[:, 32:48], r16c))
            bld.op("vector", lambda E: E.tensor_copy(el_all[:, 48:64], r16g))
            bld.op("vector", lambda E: E.tensor_copy(el_all[:, 64:80], r16e))
            bld.op("vector", lambda E: E.tensor_copy(el_all[:, 80:96], r16f))
            bld.op("vector", lambda E: E.tensor_copy(el_all[:, 96:112], r16d), inc=V)
            dbg_dump(DEBUG_TAG if DEBUG_TAG in ("ffn_rows","ffn_rows2") else "ffn_rows", el_all[:, 0:112], 112)
        comb_all = arena.take_f32(16 * 16)
        for i in range(16):
            bld.op("vector", lambda E, i=i: E.tensor_single_scalar(
                out=comb_all[:, i * 16:(i + 1) * 16], in_=lt_tm[:, i * 16:(i + 1) * 16],
                scalar=mx8_all[:, i * 8 + 1:i * 8 + 2], op=ALU.is_ge))
        for i in range(16):
            bld.op("vector", lambda E, i=i: E.tensor_mul(
                comb_all[:, i * 16:(i + 1) * 16], el_all[:, i * 16:(i + 1) * 16],
                comb_all[:, i * 16:(i + 1) * 16]))
        for i in range(16):
            bld.op("vector", lambda E, i=i: E.tensor_single_scalar(
                out=comb_all[:, i * 16:(i + 1) * 16], in_=comb_all[:, i * 16:(i + 1) * 16],
                scalar=r16d[:, i:i + 1], op=ALU.mult), inc=(V if i == 15 else None))
        bld.wait("tensor", V, V.n)
        pe_guard()
        for i in range(16):
            bci, bank_c = banks.acquire()
            pe_c = bld.op("tensor", lambda E, i=i, bank_c=bank_c: E.transpose(
                bank_c[0:16, 0:P], comb_all[:, i * 16:(i + 1) * 16], idn_sb[:]), inc=PE)
            bld.wait("vector", PE, pe_c)
            v_cf = bld.op("vector", lambda E, i=i, bank_c=bank_c: E.tensor_copy(
                comb_fm[0:16, i * P:(i + 1) * P], bank_c[0:16, 0:P]), inc=V)
            banks.release(bci, V, v_cf)
        v_comb = V.n
        if s == 0:
            dbg_dump("ffn_el", el_all, 256)
            dbg_dump("ffn_r16f", r16f, 16)
            dbg_dump("ffn_e21", r16g, 16)
            dbg_dump("ffn_rden", r16d, 16)
        if s == 0 and DEBUG_TAG == "ffn_comb":
            bld.op("vector", lambda E: E.tensor_copy(el_all, comb_fm[0:16, 0:256].bitcast(F32).bitcast(BF16)) if False else E.tensor_copy(lt_tm[0:16, 0:256], comb_fm[0:16, 0:256]), inc=V)
            dbg_dump("ffn_comb", lt_tm, 256)
        arena.pop()
        arena.push()
        h_unit = arena.take_bf16(4 * T)
        t1 = [arena.take_f32(CH) for _ in range(2)]
        t1use = [0, 0]

        units = [("s", [0]), ("s", [1])] + [("r", [2 * p, 2 * p + 1]) for p in range(8)]

        def unit_loads(kind, experts):
            loads = []
            if kind == "r":
                for j, e in enumerate(experts):
                    loads.append((j * 4096, 4096, wgu_r[e].rearrange("(k p) m -> p k m", p=128)))
                for j, e in enumerate(experts):
                    loads.append((8192 + j * 2048, 2048, wd_r[e].rearrange("(k p) m -> p k m", p=128)))
            else:
                e = experts[0]
                loads.append((0, 8192, wgu_s[e].rearrange("(k p) m -> p k m", p=128)))
                loads.append((8192, 4096, wd_s[e].rearrange("(k p) m -> p k m", p=128)))
            return loads

        wd_units = []
        for u in range(2):
            wd_units.append(wload_half(u % 2, unit_loads(*units[u])))

        for u, (kind, experts) in enumerate(units):
            half = u % 2
            base = half * RING_H
            bld.wait("tensor", WD, wd_units[u])
            pe_guard()
            if DEBUG_TAG == "ffn_ring" and s == 0 and u == 0:
                bld.wait("sync", WD, wd_units[u])
                dbg_dump("ffn_ring", w_ring[:, 0:16384].bitcast(F32), 8192)
            if u == 0:
                bld.wait("tensor", V, v_comb)
            nmid = 2 if kind == "r" else 4
            for j, e in enumerate(experts):
                gu_base = base + j * 4096 if kind == "r" else base
                wd_base = base + 8192 + j * 2048 if kind == "r" else base + 8192
                for n in range(NCHUNK):
                    bank_cb = None
                    if kind == "r":
                        bcbi, bank_cb = banks.acquire()
                        bld.op("tensor", lambda E, e=e, n=n, bank_cb=bank_cb: E.matmul(
                            bank_cb[:, 0:CH], onehot_sb[0:16, e * P:(e + 1) * P],
                            comb_fm[0:16, n * CH:(n + 1) * CH], start=True, stop=True),
                            inc=PE)
                    for mh in range(nmid):
                        bgi, bank_g = banks.acquire()
                        for k in range(G):
                            bld.op("tensor", lambda E, k=k, mh=mh, n=n, gu_base=gu_base, nmid=nmid, bank_g=bank_g: E.matmul(
                                bank_g[:, 0:CH],
                                w_ring[:, gu_base + k * (nmid * 256) + mh * P: gu_base + k * (nmid * 256) + (mh + 1) * P],
                                xn_mv(k, n), start=(k == 0), stop=(k == G - 1)),
                                inc=(PE if k == G - 1 else None))
                        pe_g = PE.n
                        bui, bank_u = banks.acquire()
                        for k in range(G):
                            bld.op("tensor", lambda E, k=k, mh=mh, n=n, gu_base=gu_base, nmid=nmid, bank_u=bank_u: E.matmul(
                                bank_u[:, 0:CH],
                                w_ring[:, gu_base + k * (nmid * 256) + (nmid + mh) * P: gu_base + k * (nmid * 256) + (nmid + mh + 1) * P],
                                xn_mv(k, n), start=(k == 0), stop=(k == G - 1)),
                                inc=(PE if k == G - 1 else None))
                        pe_u = PE.n
                        slot = (n * nmid + mh) % 2
                        ts = t1[slot]
                        bld.wait("scalar", PE, pe_g)
                        bld.wait("scalar", V, t1use[slot])
                        s_t1 = bld.op("scalar", lambda E, ts=ts, bank_g=bank_g: E.activation(
                            out=ts, in_=bank_g[:, 0:CH], func=ACT.Silu), inc=S)
                        banks.release(bgi, S, s_t1)
                        if DEBUG_TAG == "ffn_t1" and s == 0 and u == 0 and n == 0 and mh == 0:
                            dbg_dump("ffn_t1", ts, CH)
                        if DEBUG_TAG in ("ffn_gps","ffn_gps2") and s == 0 and u == 0 and n == 0 and mh == 0:
                            gf32 = arena.take_f32(CH)
                            bld.wait("vector", PE, pe_g)
                            bld.op("vector", lambda E, gf32=gf32, bank_g=bank_g: E.tensor_copy(gf32, bank_g[:, 0:CH]), inc=V)
                            dbg_dump(DEBUG_TAG, gf32, CH)
                        bld.dwait("vector", S, s_t1)
                        if kind == "r":
                            bld.op("vector", lambda E, ts=ts, bank_cb=bank_cb: E.tensor_mul(
                                ts, ts, bank_cb[:, 0:CH]))
                        bld.wait("vector", PE, pe_u)
                        gm = j * 2 + mh if kind == "r" else mh
                        v_h = bld.op("vector", lambda E, ts=ts, gm=gm, n=n, bank_u=bank_u: E.tensor_mul(
                            h_unit[:, gm * T + n * CH: gm * T + (n + 1) * CH],
                            ts, bank_u[:, 0:CH]), inc=V)
                        banks.release(bui, V, v_h)
                        t1use[slot] = v_h
                    if bank_cb is not None:
                        banks.release(bcbi, V, V.n)
            v_hdone = V.n
            if s == 0 and (DEBUG_TAG or "").startswith("ffn_h") and u == int(DEBUG_TAG[5:6]):
                hf32 = arena.take_f32(2048)
                bld.op("vector", lambda E, hf32=hf32: E.tensor_copy(hf32, h_unit[:, 0:T]), inc=V)
                dbg_dump(DEBUG_TAG, hf32, 2048)
            bld.wait("tensor", V, v_hdone)
            for o in range(G):
                for n in range(NCHUNK):
                    bdi, bank_d = banks.acquire()
                    for k in range(4):
                        bld.op("tensor", lambda E, k=k, o=o, n=n, base=base, bank_d=bank_d: E.matmul(
                            bank_d[:, 0:CH],
                            w_ring[:, base + 8192 + k * 1024 + o * P: base + 8192 + k * 1024 + (o + 1) * P],
                            h_unit[:, k * T + n * CH: k * T + (n + 1) * CH],
                            start=(k == 0), stop=(k == 3)),
                            inc=(PE if k == 3 else None))
                    pe_d = PE.n
                    bld.wait("vector", PE, pe_d)
                    v_d = bld.op("vector", lambda E, o=o, n=n, bank_d=bank_d: E.tensor_add(
                        x_fm[:, o * T + n * CH: o * T + (n + 1) * CH],
                        x_fm[:, o * T + n * CH: o * T + (n + 1) * CH],
                        bank_d[:, 0:CH]), inc=V)
                    banks.release(bdi, V, v_d)
            wring_half[half] = (PE, PE.n)
            if u + 2 < len(units):
                wd_units.append(wload_half((u + 2) % 2, unit_loads(*units[u + 2])))
        arena.pop()
        arena.pop()

    def attb_ln_rows(c0, CHB, xpb, xsb, tmp_f, double_ln, src_fn=None, want_r1=False):
        if src_fn is None:
            xc = lambda g: x_fm[:, g * T + c0: g * T + c0 + CHB]
            for g in range(G):
                bld.op("vector", lambda E, g=g: E.tensor_copy(
                    xpb[:, g * CHB:(g + 1) * CHB], xc(g)), inc=(V if g == G - 1 else None))
            stats_src = lambda g: xpb[:, g * CHB:(g + 1) * CHB]
            sq_src = xc
        else:
            stats_src = src_fn
            sq_src = src_fn
        bld.dwait("scalar", V, V.n)
        for g in range(G):
            bld.op("scalar", lambda E, g=g: E.activation(
                out=xsb[:, g * CHB:(g + 1) * CHB], in_=sq_src(g), func=ACT.Square),
                inc=(S if g == G - 1 else None))
        b1i, b1 = banks.acquire()
        bld.wait("tensor", V, V.n)
        for g in range(G):
            bld.op("tensor", lambda E, g=g, b1=b1: E.matmul(
                b1[0:1, 0:CHB], ones_bf[:, 0:1], stats_src(g),
                start=(g == 0), stop=(g == G - 1)), inc=(PE if g == G - 1 else None))
        b2i, b2 = banks.acquire()
        bld.wait("tensor", S, S.n)
        for g in range(G):
            bld.op("tensor", lambda E, g=g, b2=b2: E.matmul(
                b2[0:1, 0:CHB], ones_bf[:, 0:1], xsb[:, g * CHB:(g + 1) * CHB],
                start=(g == 0), stop=(g == G - 1)), inc=(PE if g == G - 1 else None))
        mu = rows[0:1, 0:CHB]
        v1 = rows[0:1, CHB:2 * CHB]
        t1 = rows[0:1, 2 * CHB:3 * CHB]
        s_row = rows[0:1, 3 * CHB:4 * CHB]
        ms_row = rows[0:1, 4 * CHB:5 * CHB]
        t2 = rows[0:1, 5 * CHB:6 * CHB]
        bld.wait("vector", PE, PE.n)
        bld.op("vector", lambda E: E.tensor_scalar_mul(mu, b1[0:1, 0:CHB], 1.0 / D))
        vv = bld.op("vector", lambda E: E.tensor_scalar_mul(v1, b2[0:1, 0:CHB], 1.0 / D), inc=V)
        banks.release(b1i, V, vv)
        banks.release(b2i, V, vv)
        bld.op("vector", lambda E: E.tensor_mul(t1, mu, mu))
        bld.op("vector", lambda E: E.tensor_sub(v1, v1, t1), inc=V)
        bld.dwait("scalar", V, V.n)
        bld.op("scalar", lambda E: E.activation(out=t1, in_=v1, func=ACT.Ln, bias=eps_col[0:1, :]))
        out = {}
        if double_ln:
            r1_row = rows[0:1, 6 * CHB:7 * CHB]
            mr1_row = rows[0:1, 7 * CHB:8 * CHB]
            bld.op("scalar", lambda E, r1_row=r1_row: E.activation(
                out=r1_row, in_=t1, func=ACT.Exp, scale=-0.5))
            s1v = bld.op("scalar", lambda E: E.activation(out=t2, in_=t1, func=ACT.Exp, scale=-1.0), inc=S)
            bld.dwait("vector", S, s1v)
            vv2 = bld.op("vector", lambda E: E.tensor_mul(t2, v1, t2), inc=V)
            bld.dwait("scalar", V, vv2)
            bld.op("scalar", lambda E: E.activation(out=t2, in_=t2, func=ACT.Ln, bias=eps_col[0:1, :]))
            s2v = bld.op("scalar", lambda E: E.activation(out=t2, in_=t2, func=ACT.Exp, scale=-0.5), inc=S)
            bld.dwait("vector", S, s2v)
            bld.op("vector", lambda E, r1_row=r1_row: E.tensor_mul(s_row, r1_row, t2))
            if want_r1:
                bld.op("vector", lambda E, r1_row=r1_row, mr1_row=mr1_row: E.scalar_tensor_tensor(
                    out=mr1_row, in0=mu, scalar=-1.0, in1=r1_row, op0=ALU.mult, op1=ALU.mult))
        else:
            s2v = bld.op("scalar", lambda E: E.activation(out=s_row, in_=t1, func=ACT.Exp, scale=-0.5), inc=S)
            bld.dwait("vector", S, s2v)
        bld.op("vector", lambda E: E.scalar_tensor_tensor(
            out=ms_row, in0=mu, scalar=-1.0, in1=s_row, op0=ALU.mult, op1=ALU.mult), inc=V)
        bld.wait("tensor", V, V.n)
        for name, row in ([("s", s_row), ("m", ms_row)] +
                          ([("r1", r1_row), ("mr1", mr1_row)] if (double_ln and want_r1) else [])):
            bi, bank = banks.acquire()
            bld.op("tensor", lambda E, bank=bank, row=row, CHB=CHB: E.matmul(
                bank[:, 0:CHB], ones_row[0:1, :], row, start=True, stop=True),
                inc=(PE if name == ("mr1" if (double_ln and want_r1) else "m") else None))
            out[name] = (bi, bank)
        return out

    def phase_attnb(s):
        CHB = 256
        L1 = CHB + 1
        HT = 512
        arena.push()
        x1_bf = arena.take_bf16(G * HT)

        pp_st = state_sb[:, 0:8]
        aa_st = state_sb[:, 8:16]
        kmaxc = state_sb[:, 16:24]
        mcol = state_sb[:, 24:32]
        negm = state_sb[:, 32:40]
        argc = state_sb[:, 40:48]
        rpc = state_sb[:, 48:56]

        bld.wait("gpsimd", V, V.n)
        bld.op("gpsimd", lambda E: E.memset(pp_st, -1e38))
        bld.op("gpsimd", lambda E: E.memset(aa_st, 0.0), inc=GP)
        bld.dwait("vector", GP, GP.n)

        for c5 in range(T // HT):
            arena.push()
            xa_pad = arena.take_f32(G * L1)
            xpb = arena.take_bf16(G * CHB)
            xsb = arena.take_bf16(G * CHB)
            tmp_f = arena.take_f32(CHB)
            dbuf = tmp_f
            xkb = arena.take_bf16(G * CHB)
            xvb = arena.take_bf16(G * CHB)
            xrb = arena.take_bf16(G * CHB)
            kbuf = arena.take_f32(G * CHB)
            vbuf = arena.take_bf16(G * CHB)
            sgbuf = arena.take_bf16(G * CHB)
            ewt = arena.take_f32(L1)
            kmb = arena.take_f32(CHB)
            d1b = arena.take_f32(L1)
            ptil = arena.take_f32(L1)
            lp = arena.take_f32(L1)
            mxb = arena.take_f32(CHB)
            cab = arena.take_f32(CHB)
            cbuf = arena.take_f32(L1)
            cdb = arena.take_f32(CHB)
            btb = arena.take_f32(L1)
            ntil = arena.take_f32(L1)
            Eb = arena.take_f32(CHB)
            denb = arena.take_f32(CHB)
            rdnb = arena.take_f32(CHB)
            kmb2 = io_st[:, 0:CHB]
            Eb2 = io_st[:, CHB:2 * CHB]
            xa_g = lambda g: xa_pad[:, g * L1:(g + 1) * L1]

            wait_halves("gpsimd", (0, 1))
            for half in range(2):
                bld.op("gpsimd", lambda E, half=half: E.dma_start(
                    out=w_ring[:, half * RING_H:(half + 1) * RING_H].rearrange(
                        "p (k m) -> p k m", k=4),
                    in_=wkvr.rearrange("(kk k p) m -> kk p k m", kk=2, p=128)[half]),
                    inc=WD, n=16)
            wd_done = WD.n
            if c5 == 0:
                bld.op("gpsimd", lambda E: E.memset(cbuf[:, 0:1], 0.0))
                bld.op("gpsimd", lambda E: E.memset(ewt[:, 0:1], 0.0), inc=GP)
                bld.dwait("vector", GP, GP.n)

            for sub in range(2):
                c0 = c5 * HT + sub * CHB
                lnb = attb_ln_rows(c0, CHB, xpb, xsb, tmp_f, True, want_r1=True)
                bank_s = lnb["s"][1]
                bank_m = lnb["m"][1]
                bld.wait("vector", PE, PE.n)
                for g in range(G):
                    if c5 == 0 and sub == 0:
                        bld.op("vector", lambda E, g=g: E.memset(xa_g(g)[:, 0:1], 0.0))
                    else:
                        bld.op("vector", lambda E, g=g: E.tensor_copy(
                            xa_g(g)[:, 0:1], xa_g(g)[:, CHB:CHB + 1]))
                for g in range(G):
                    bld.op("vector", lambda E, g=g, bank_s=bank_s, c0=c0: E.tensor_mul(
                        tmp_f, x_fm[:, g * T + c0: g * T + c0 + CHB], bank_s[:, 0:CHB]))
                    bld.op("vector", lambda E, g=g, bank_m=bank_m: E.tensor_add(
                        tmp_f, tmp_f, bank_m[:, 0:CHB]))
                    bld.op("vector", lambda E, g=g: E.tensor_scalar(
                        out=xa_g(g)[:, 1:], in0=tmp_f, scalar1=vcol("ln1g", g),
                        scalar2=vcol("ln1b", g), op0=ALU.mult, op1=ALU.add),
                        inc=(V if g == G - 1 else None))
                banks.release(lnb["s"][0], V, V.n)
                banks.release(lnb["m"][0], V, V.n)
                for g in range(G):
                    bld.op("vector", lambda E, g=g: E.tensor_sub(
                        dbuf, xa_g(g)[:, 1:], xa_g(g)[:, 0:CHB]))
                    for name, buf in (("rk", xkb), ("rv", xvb), ("rr", xrb)):
                        bld.op("vector", lambda E, g=g, name=name, buf=buf: E.scalar_tensor_tensor(
                            out=buf[:, g * CHB:(g + 1) * CHB], in0=dbuf, scalar=vcol(name, g),
                            in1=xa_g(g)[:, 0:CHB], op0=ALU.mult, op1=ALU.add),
                            inc=(V if g == G - 1 and name == "rr" else None))
                v_mix = V.n
                bld.wait("tensor", WD, wd_done)
                pe_guard()
                bld.wait("tensor", V, v_mix)
                for pi, (mvb, ebuf) in enumerate(((xkb, kbuf), (xvb, vbuf), (xrb, sgbuf))):
                    for m in range(G):
                        bki, bank_k = banks.acquire()
                        for k in range(G):
                            kk, kr = k // 4, k % 4
                            bld.op("tensor", lambda E, k=k, kk=kk, kr=kr, pi=pi, m=m, mvb=mvb, bank_k=bank_k: E.matmul(
                                bank_k[:, 0:CHB],
                                w_ring[:, kk * RING_H + kr * 3072 + pi * 1024 + m * P:
                                       kk * RING_H + kr * 3072 + pi * 1024 + (m + 1) * P],
                                mvb[:, k * CHB:(k + 1) * CHB],
                                start=(k == 0), stop=(k == G - 1)),
                                inc=(PE if k == G - 1 else None))
                        pe_k = PE.n
                        if pi == 0:
                            bld.wait("vector", PE, pe_k)
                            v_e = bld.op("vector", lambda E, m=m, ebuf=ebuf, bank_k=bank_k: E.tensor_copy(
                                ebuf[:, m * CHB:(m + 1) * CHB], bank_k[:, 0:CHB]), inc=V)
                            banks.release(bki, V, v_e)
                        elif pi == 1:
                            bld.wait("vector", PE, pe_k)
                            v_e = bld.op("vector", lambda E, m=m, ebuf=ebuf, bank_k=bank_k: E.tensor_copy(
                                ebuf[:, m * CHB:(m + 1) * CHB], bank_k[:, 0:CHB]), inc=V)
                            banks.release(bki, V, v_e)
                        else:
                            bld.wait("scalar", PE, pe_k)
                            s_e = bld.op("scalar", lambda E, m=m, ebuf=ebuf, bank_k=bank_k: E.activation(
                                out=ebuf[:, m * CHB:(m + 1) * CHB], in_=bank_k[:, 0:CHB],
                                func=ACT.Sigmoid), inc=S)
                            banks.release(bki, S, s_e)
                bank_r1 = lnb["r1"][1]
                bank_mr1 = lnb["mr1"][1]
                bld.dwait("vector", S, S.n)
                for g in range(G):
                    bld.op("vector", lambda E, g=g: E.reduce_max(
                        out=kmaxc[:, g:g + 1], in_=kbuf[:, g * CHB:(g + 1) * CHB], axis=AXX))
                for g in range(G):
                    bld.op("vector", lambda E, g=g: E.tensor_max(
                        mcol[:, g:g + 1], kmaxc[:, g:g + 1], pp_st[:, g:g + 1]))
                for g in range(G):
                    bld.op("vector", lambda E, g=g: E.tensor_sub(
                        argc[:, g:g + 1], pp_st[:, g:g + 1], mcol[:, g:g + 1]))
                for g in range(G):
                    bld.op("vector", lambda E, g=g: E.tensor_scalar_mul(
                        negm[:, g:g + 1], mcol[:, g:g + 1], -1.0), inc=(V if g == G - 1 else None))
                km_of = lambda g: (kmb if g % 2 == 0 else kmb2)
                Eb_of = lambda g: (Eb if g % 2 == 0 else Eb2)
                v_km = [None] * G
                s1done = [None] * G

                def emit_stage1_v(g):
                    kslice = kbuf[:, g * CHB:(g + 1) * CHB]
                    v_km[g] = bld.op("vector", lambda E, g=g, kslice=kslice: E.tensor_single_scalar(
                        out=km_of(g), in_=kslice, scalar=negm[:, g:g + 1], op=ALU.add), inc=V)
                    bld.op("vector", lambda E, g=g: E.tensor_scalar(
                        out=ewt[:, 1:], in0=km_of(g), scalar1=0.0, scalar2=vcol("ew", g),
                        op0=ALU.mult, op1=ALU.add))

                def emit_stage1_s(g):
                    bld.dwait("scalar", V, v_km[g])
                    bld.op("scalar", lambda E, g=g: E.activation(
                        out=d1b[:, 1:], in_=km_of(g), func=ACT.Exp, bias=vcol("w", g)))
                    bld.op("scalar", lambda E, g=g: E.activation(
                        out=Eb_of(g), in_=km_of(g), func=ACT.Exp, bias=vcol("u", g)))
                    s1done[g] = bld.op("scalar", lambda E, g=g: E.activation(
                        out=d1b[:, 0:1], in_=argc[:, g:g + 1], func=ACT.Exp), inc=S)

                emit_stage1_v(0)
                emit_stage1_s(0)
                for g in range(G):
                    vslice = vbuf[:, g * CHB:(g + 1) * CHB]
                    bld.dwait("vector", S, s1done[g])
                    bld.op("vector", lambda E, g=g: E.tensor_tensor_scan(
                        out=ptil, data0=ewt, data1=d1b, initial=0.0,
                        op0=ALU.mult, op1=ALU.add))
                    bld.op("vector", lambda E, g=g: E.tensor_add(denb, ptil[:, 0:CHB], Eb_of(g)))
                    v_scan = bld.op("vector", lambda E, g=g: E.tensor_mul(
                        btb[:, 0:1], aa_st[:, g:g + 1], d1b[:, 0:1]), inc=V)
                    if g + 1 < G:
                        emit_stage1_v(g + 1)
                    bld.op("vector", lambda E, g=g, c0=c0, bank_r1=bank_r1: E.tensor_mul(
                        tmp_f, x_fm[:, g * T + c0: g * T + c0 + CHB], bank_r1[:, 0:CHB]))
                    bld.op("vector", lambda E, bank_mr1=bank_mr1: E.tensor_add(
                        tmp_f, tmp_f, bank_mr1[:, 0:CHB]))
                    bld.dwait("scalar", V, v_scan)
                    bld.op("scalar", lambda E: E.activation(out=lp, in_=ptil, func=ACT.Ln))
                    bld.op("scalar", lambda E, g=g: E.activation(
                        out=rpc[:, g:g + 1], in_=lp[:, CHB:CHB + 1], func=ACT.Exp, scale=-1.0))
                    bld.op("scalar", lambda E: E.activation(out=cdb, in_=denb, func=ACT.Ln))
                    s_lp = bld.op("scalar", lambda E: E.activation(
                        out=rdnb, in_=cdb, func=ACT.Exp, scale=-1.0), inc=S)
                    bld.dwait("vector", S, s_lp)
                    bld.op("vector", lambda E, g=g: E.tensor_max(mxb, lp[:, 0:CHB], km_of(g)))
                    bld.op("vector", lambda E: E.tensor_sub(cab, lp[:, 1:], mxb))
                    v_c2 = bld.op("vector", lambda E, g=g: E.tensor_add(mxb, cab, km_of(g)), inc=V)
                    bld.dwait("scalar", V, v_c2)
                    bld.op("scalar", lambda E: E.activation(out=cbuf[:, 1:], in_=cab, func=ACT.Exp))
                    s_c = bld.op("scalar", lambda E: E.activation(out=cdb, in_=mxb, func=ACT.Exp), inc=S)
                    if g + 1 < G:
                        emit_stage1_s(g + 1)
                    bld.dwait("vector", S, s_c)
                    bld.op("vector", lambda E, vslice=vslice: E.tensor_mul(btb[:, 1:], cdb, vslice))
                    bld.op("vector", lambda E: E.tensor_tensor_scan(
                        out=ntil, data0=cbuf, data1=btb, initial=0.0,
                        op0=ALU.mult, op1=ALU.add))
                    bld.op("vector", lambda E, g=g, vslice=vslice: E.tensor_mul(
                        Eb_of(g), Eb_of(g), vslice))
                    bld.op("vector", lambda E, g=g: E.tensor_add(Eb_of(g), ntil[:, 0:CHB], Eb_of(g)))
                    bld.op("vector", lambda E, g=g: E.tensor_mul(Eb_of(g), Eb_of(g), rdnb))
                    bld.op("vector", lambda E, g=g: E.tensor_mul(
                        Eb_of(g), Eb_of(g), sgbuf[:, g * CHB:(g + 1) * CHB]))
                    bld.op("vector", lambda E, g=g, sub=sub: E.tensor_add(
                        x1_bf[:, g * HT + sub * CHB: g * HT + sub * CHB + CHB], tmp_f, Eb_of(g)))
                    bld.op("vector", lambda E, g=g: E.tensor_add(
                        pp_st[:, g:g + 1], mcol[:, g:g + 1], lp[:, CHB:CHB + 1]))
                    bld.op("vector", lambda E, g=g: E.tensor_mul(
                        aa_st[:, g:g + 1], ntil[:, CHB:CHB + 1], rpc[:, g:g + 1]), inc=V)
                banks.release(lnb["r1"][0], V, V.n)
                banks.release(lnb["mr1"][0], V, V.n)
            mark_wring_free()
            arena.pop()
            arena.push()
            xck_bf = arena.take_bf16(G * HT)
            xcr_bf = arena.take_bf16(G * HT)
            sigrc = arena.take_f32(G * HT)
            arena.push()
            xb_pad = arena.take_f32(G * (HT + 1))
            xsb2 = arena.take_bf16(G * HT)
            tmp_h = arena.take_f32(HT)
            xb_g = lambda g: xb_pad[:, g * (HT + 1):(g + 1) * (HT + 1)]
            x1_g = lambda g: x1_bf[:, g * HT:(g + 1) * HT]

            wd_rc = wload_half(0, [(0, 8192, wcr.rearrange("(k p) m -> p k m", p=128))])

            def hg_loads(hg):
                return [(0, 4096, wck[:, hg * 512:(hg + 1) * 512].rearrange(
                            "(k p) m -> p k m", p=128)),
                        (4096, 4096, wcv[hg * 512:(hg + 1) * 512, :].rearrange(
                            "(k p) m -> p k m", p=128))]

            wd_hg = [wload_half(1, hg_loads(0))]
            lnb2 = attb_ln_rows(0, HT, None, xsb2, tmp_h, False, src_fn=x1_g)
            bank_s2 = lnb2["s"][1]
            bank_m2 = lnb2["m"][1]
            bld.wait("vector", PE, PE.n)
            for g in range(G):
                if c5 == 0:
                    bld.op("vector", lambda E, g=g: E.memset(xb_g(g)[:, 0:1], 0.0))
                else:
                    bld.op("vector", lambda E, g=g: E.tensor_copy(
                        xb_g(g)[:, 0:1], state_sb[:, 56 + g:57 + g]))
            for g in range(G):
                bld.op("vector", lambda E, g=g, bank_s2=bank_s2: E.tensor_mul(
                    tmp_h, x1_g(g), bank_s2[:, 0:HT]))
                bld.op("vector", lambda E, g=g, bank_m2=bank_m2: E.tensor_add(
                    tmp_h, tmp_h, bank_m2[:, 0:HT]))
                bld.op("vector", lambda E, g=g: E.tensor_scalar(
                    out=xb_g(g)[:, 1:], in0=tmp_h, scalar1=vcol("ln2g", g),
                    scalar2=vcol("ln2b", g), op0=ALU.mult, op1=ALU.add),
                    inc=(V if g == G - 1 else None))
            banks.release(lnb2["s"][0], V, V.n)
            banks.release(lnb2["m"][0], V, V.n)
            for g in range(G):
                bld.op("vector", lambda E, g=g: E.tensor_copy(
                    state_sb[:, 56 + g:57 + g], xb_g(g)[:, HT:HT + 1]))
            for g in range(G):
                bld.op("vector", lambda E, g=g: E.tensor_sub(
                    tmp_h, xb_g(g)[:, 1:], xb_g(g)[:, 0:HT]))
                bld.op("vector", lambda E, g=g: E.scalar_tensor_tensor(
                    out=xck_bf[:, g * HT:(g + 1) * HT], in0=tmp_h, scalar=vcol("crk", g),
                    in1=xb_g(g)[:, 0:HT], op0=ALU.mult, op1=ALU.add))
                bld.op("vector", lambda E, g=g: E.scalar_tensor_tensor(
                    out=xcr_bf[:, g * HT:(g + 1) * HT], in0=tmp_h, scalar=vcol("crr", g),
                    in1=xb_g(g)[:, 0:HT], op0=ALU.mult, op1=ALU.add),
                    inc=(V if g == G - 1 else None))
            v_mix2 = V.n
            bld.wait("tensor", WD, wd_rc)
            pe_guard()
            bld.wait("tensor", V, v_mix2)
            for m in range(G):
                bri, bank_r = banks.acquire()
                for k in range(G):
                    bld.op("tensor", lambda E, k=k, m=m, bank_r=bank_r: E.matmul(
                        bank_r[:, 0:HT], w_ring[:, k * 1024 + m * P: k * 1024 + (m + 1) * P],
                        xcr_bf[:, k * HT:(k + 1) * HT], start=(k == 0), stop=(k == G - 1)),
                        inc=(PE if k == G - 1 else None))
                pe_rc = PE.n
                bld.wait("scalar", PE, pe_rc)
                s_rc = bld.op("scalar", lambda E, m=m, bank_r=bank_r: E.activation(
                    out=sigrc[:, m * HT:(m + 1) * HT], in_=bank_r[:, 0:HT],
                    func=ACT.Sigmoid), inc=S)
                banks.release(bri, S, s_rc)
            wring_half[0] = (PE, PE.n)
            wd_hg.append(wload_half(0, hg_loads(1)))
            arena.pop()
            arena.push()
            kc_bf = arena.take_bf16(4 * HT)
            tmp2 = arena.take_f32(HT)
            vc = arena.take_f32(G * HT)
            for hg in range(8):
                half = (hg + 1) % 2
                base = half * RING_H
                bld.wait("tensor", WD, wd_hg[hg])
                pe_guard()
                for mh in range(4):
                    bki, bank_k = banks.acquire()
                    for k in range(G):
                        bld.op("tensor", lambda E, k=k, mh=mh, base=base, bank_k=bank_k: E.matmul(
                            bank_k[:, 0:HT],
                            w_ring[:, base + k * 512 + mh * P: base + k * 512 + (mh + 1) * P],
                            xck_bf[:, k * HT:(k + 1) * HT], start=(k == 0), stop=(k == G - 1)),
                            inc=(PE if k == G - 1 else None))
                    pe_kc = PE.n
                    bld.wait("scalar", PE, pe_kc)
                    bld.op("scalar", lambda E, bank_k=bank_k: E.activation(
                        out=tmp2, in_=bank_k[:, 0:HT], func=ACT.Relu))
                    s_kc = bld.op("scalar", lambda E, mh=mh: E.activation(
                        out=kc_bf[:, mh * HT:(mh + 1) * HT], in_=tmp2, func=ACT.Square), inc=S)
                    banks.release(bki, S, s_kc)
                s_kcd = S.n
                bld.wait("tensor", S, s_kcd)
                for o in range(G):
                    bvi, bank_v = banks.acquire()
                    for k2 in range(4):
                        bld.op("tensor", lambda E, k2=k2, o=o, base=base, bank_v=bank_v: E.matmul(
                            bank_v[:, 0:HT],
                            w_ring[:, base + 4096 + k2 * 1024 + o * P: base + 4096 + k2 * 1024 + (o + 1) * P],
                            kc_bf[:, k2 * HT:(k2 + 1) * HT], start=(k2 == 0), stop=(k2 == 3)),
                            inc=(PE if k2 == 3 else None))
                    pe_vc = PE.n
                    bld.wait("vector", PE, pe_vc)
                    if hg == 0:
                        v_vc = bld.op("vector", lambda E, o=o, bank_v=bank_v: E.tensor_copy(
                            vc[:, o * HT:(o + 1) * HT], bank_v[:, 0:HT]), inc=V)
                    else:
                        v_vc = bld.op("vector", lambda E, o=o, bank_v=bank_v: E.tensor_add(
                            vc[:, o * HT:(o + 1) * HT], vc[:, o * HT:(o + 1) * HT],
                            bank_v[:, 0:HT]), inc=V)
                    banks.release(bvi, V, v_vc)
                wring_half[half] = (PE, PE.n)
                if hg + 2 < 8:
                    wd_hg.append(wload_half((hg + 2 + 1) % 2, hg_loads(hg + 2)))
            bld.dwait("vector", S, S.n)
            for g in range(G):
                bld.op("vector", lambda E, g=g: E.tensor_mul(
                    vc[:, g * HT:(g + 1) * HT], vc[:, g * HT:(g + 1) * HT],
                    sigrc[:, g * HT:(g + 1) * HT]))
                bld.op("vector", lambda E, g=g, c5=c5: E.tensor_add(
                    x_fm[:, g * T + c5 * HT: g * T + (c5 + 1) * HT],
                    x_fm[:, g * T + c5 * HT: g * T + (c5 + 1) * HT],
                    vc[:, g * HT:(g + 1) * HT]))
                bld.op("vector", lambda E, g=g, c5=c5: E.tensor_add(
                    x_fm[:, g * T + c5 * HT: g * T + (c5 + 1) * HT],
                    x_fm[:, g * T + c5 * HT: g * T + (c5 + 1) * HT],
                    x1_g(g)), inc=V)
            mark_wring_free()
            arena.pop()
            arena.pop()
        arena.pop()

    for s in range(SEQ_PER_CORE):
        load_seq(s)
        if "attn_a" in phases:
            phase_attn_a(s)
        if "ffn1" in phases:
            phase_ffn(s)
        if "attnb" in phases:
            phase_attnb(s)
        if "ffn2" in phases:
            phase_ffn(s)
        store_seq(s)

    bld.wait("sync", OD, OD.n)

    with nc.Block() as block:
        bld.replay(block)
    return nc


def prep_inputs(inputs):
    def c(a):
        return np.ascontiguousarray(a, dtype=np.float32)

    Wz = inputs["L0_attn_a_Wz"]
    Wh = inputs["L0_attn_a_Wh"]
    wzh = c(np.concatenate([np.transpose(Wz, (1, 0, 2)).reshape(D, D),
                            np.transpose(Wh, (1, 0, 2)).reshape(D, D)], axis=1))
    wg_a = c(inputs["L0_attn_a_Wg"])

    def packvec(v):
        return np.asarray(v, np.float32).reshape(8, 128).T

    names = ["bz", "bh", "bg", "w", "u", "rk", "rv", "rr", "crk", "crr",
             "ln1g", "ln1b", "ln2g", "ln2b", "ew"]
    src = {
        "bz": np.asarray(inputs["L0_attn_a_bz"]).reshape(-1),
        "bh": np.asarray(inputs["L0_attn_a_bh"]).reshape(-1),
        "bg": inputs["L0_attn_a_bg"],
        "w": inputs["L0_attn_b_w"], "u": inputs["L0_attn_b_u"],
        "rk": inputs["L0_attn_b_rk"], "rv": inputs["L0_attn_b_rv"],
        "rr": inputs["L0_attn_b_rr"], "crk": inputs["L0_attn_b_crk"],
        "crr": inputs["L0_attn_b_crr"],
        "ln1g": inputs["L0_attn_b_ln1_g"], "ln1b": inputs["L0_attn_b_ln1_b"],
        "ln2g": inputs["L0_attn_b_ln2_g"], "ln2b": inputs["L0_attn_b_ln2_b"],
        "ew": np.zeros(D, np.float32),
    }
    vecs = np.zeros((P, 8 * 16), np.float32)
    for i, n in enumerate(names):
        vecs[:, 8 * i:8 * i + 8] = packvec(src[n])

    wgu_r = c(np.concatenate([inputs["L0_ffn_wg_r"], inputs["L0_ffn_wu_r"]], axis=2))
    wd_r = c(inputs["L0_ffn_wd_r"])
    wgu_s = c(np.concatenate([inputs["L0_ffn_wg_s"], inputs["L0_ffn_wu_s"]], axis=2))
    wd_s = c(inputs["L0_ffn_wd_s"])
    wkvr = c(np.concatenate([np.asarray(inputs["L0_attn_b_Wk"]).T,
                             np.asarray(inputs["L0_attn_b_Wv"]).T,
                             np.asarray(inputs["L0_attn_b_Wr"]).T], axis=1))
    wck = c(np.asarray(inputs["L0_attn_b_Wck"]).T)
    wcv = c(np.asarray(inputs["L0_attn_b_Wcv"]).T)
    wcr = c(np.asarray(inputs["L0_attn_b_Wcr"]).T)

    onehot = np.zeros((16, 2048), np.float32)
    for e in range(16):
        onehot[e, e * 128:(e + 1) * 128] = 1.0

    shared = dict(
        wzh=wzh, wg_a=wg_a, vecs=vecs, idn=np.eye(P, dtype=np.float32),
        rw=c(inputs["L0_ffn_router_w"]), wgu_r=wgu_r, wd_r=wd_r,
        wgu_s=wgu_s, wd_s=wd_s, wkvr=wkvr, wck=wck, wcv=wcv, wcr=wcr,
        onehot=onehot,
    )
    x = np.asarray(inputs["L0_attn_a_x"], np.float32)
    shards = np.split(np.ascontiguousarray(x), N_CORES, axis=0)
    in_maps = [dict(shared, x=np.ascontiguousarray(sh)) for sh in shards]
    return in_maps


def kernel(**inputs):
    in_maps = prep_inputs(inputs)
    nc = build_program()
    res = run_bass_kernel_spmd(nc, in_maps, core_ids=list(range(N_CORES)))
    return np.concatenate([r["y"] for r in res.results], axis=0)


if __name__ == "__main__":
    pass
